# version 10
# baseline (speedup 1.0000x reference)
import numpy as np
import concourse.bass as bass
import concourse.mybir as mybir
from concourse.alu_op_type import AluOpType as ALU
from concourse.bass_utils import run_bass_kernel_spmd

F32 = mybir.dt.float32
BF16 = mybir.dt.bfloat16
I32 = mybir.dt.int32
AF = mybir.ActivationFunctionType

MAGIC = float(np.float32(1.5 * 2 ** 23))
TWO_PI = float(2 * np.pi)
RSQRT_MAGIC = 0x5F3759DF

NCORE = 8
BSEQ = 2
L = 2048
D = 1024
CT = 4128
DI = 2048
TILE = 512
NSUB = TILE // 128
NT = L // TILE
NTT = BSEQ * NT
NKV = 16
NGA = 16
NFC = 8
XNW = 516

ENGINES = ("T", "V", "S", "SY", "G")


class Sched:

    def __init__(self, nc):
        self.nc = nc
        self.cmds = {e: [] for e in ENGINES}
        self.seq = {e: 0 for e in ENGINES}
        self.clock = {e: {} for e in ENGINES}
        self.tokc = {}
        self.last_w = {}
        self.readers = {}
        self.chain = {e: "ch" + e for e in ENGINES}
        self.dma_use = {}
        self.dma_n = 0
        self.dma_g = 0
        self.out_tokens = []
        self.sem_names = set(self.chain.values())

    def _wait(self, eng, tok, waits):
        semk, val = tok
        cl = self.clock[eng]
        if cl.get(semk, 0) >= val:
            return
        waits.append((semk, val))
        tc = self.tokc.get(tok)
        if tc:
            for k, v in tc.items():
                if cl.get(k, 0) < v:
                    cl[k] = v
        if cl.get(semk, 0) < val:
            cl[semk] = val

    def emit(self, eng, fn, reads=(), writes=(), dma=None, first=True, last=True):
        waits = []
        toks = []
        if first:
            for r in writes:
                t = self.last_w.get(r)
                if t is not None:
                    toks.append(t)
                toks.extend(self.readers.get(r, ()))
        for r in reads:
            t = self.last_w.get(r)
            if t is not None:
                toks.append(t)
        need = {}
        for semk, val in toks:
            if need.get(semk, 0) < val:
                need[semk] = val
        for semk in sorted(need):
            self._wait(eng, (semk, need[semk]), waits)

        if dma is not None:
            if eng == "G":
                name = "dg%d" % (self.dma_g % 8)
                self.dma_g += 1
            else:
                name = "dp%d" % (self.dma_n % 56)
                self.dma_n += 1
            self.sem_names.add(name)
            prev = self.dma_use.get(name, 0)
            if prev > 0:
                self._wait(eng, (name, prev), waits)
            self.dma_use[name] = prev + 16
            tok = (name, self.dma_use[name])
            incs = [(name, 16)]
            if dma == "dO":
                self.out_tokens.append(tok)
        else:
            self.seq[eng] += 1
            tok = (self.chain[eng], self.seq[eng])
            incs = [(self.chain[eng], 1)]
        tc = dict(self.clock[eng])
        tc[tok[0]] = tok[1]
        self.tokc[tok] = tc
        self.cmds[eng].append((waits, fn, incs))

        for r in reads:
            self.readers.setdefault(r, []).append(tok)
        if last:
            for r in writes:
                self.last_w[r] = tok
                self.readers[r] = []
        return tok

    def raw_wait(self, eng, semk, val):
        self.cmds[eng].append(([(semk, val)], None, []))
        cl = self.clock[eng]
        if cl.get(semk, 0) < val:
            cl[semk] = val

    def build(self, nc):
        sems = {name: nc.alloc_semaphore(name) for name in sorted(self.sem_names)}
        with nc.Block() as block:
            def mk(engname):
                def body(eobj):
                    for waits, fn, incs in self.cmds[engname]:
                        for semk, val in waits:
                            eobj.wait_ge(sems[semk], val)
                        if fn is not None:
                            inst = fn(eobj)
                            for semk, d in incs:
                                inst = inst.then_inc(sems[semk], d)
                return body
            block.tensor(mk("T"))
            block.vector(mk("V"))
            block.scalar(mk("S"))
            block.sync(mk("SY"))
            block.gpsimd(mk("G"))
        return sems


def _host_constants(ins):
    ck = np.ascontiguousarray(ins["L0_attn_conv_k"]).astype(np.float32, copy=False)
    c = {}
    c["ipw"] = np.ascontiguousarray(ins["L0_attn_in_proj_w"], np.float32)

    diag = np.zeros((128, 32, 4, 128), np.float32)
    ar = np.arange(128)
    for cc in range(32):
        for j in range(4):
            diag[ar, cc, j, ar] = ck[j, 0, cc * 128 + ar]
    c["diag"] = diag.reshape(128, 32 * 4 * 128)

    diag_s = np.zeros((32, 4, 32), np.float32)
    a32 = np.arange(32)
    for j in range(4):
        diag_s[a32, j, a32] = ck[j, 0, 4096 + a32]
    c["diag_s"] = diag_s.reshape(32, 128)

    th = np.ascontiguousarray(ins["L0_attn_theta"], np.float32).reshape(32, 64)
    c["theta_pc"] = th.reshape(16, 2, 64).transpose(1, 2, 0).reshape(128, 16).copy()

    cb = np.ascontiguousarray(ins["L0_attn_conv_b"], np.float32)
    bd = np.zeros((128, 33), np.float32)
    bd[:, :32] = cb[:4096].reshape(32, 128).T
    bd[:32, 32] = cb[4096:4128]
    c["bdev"] = bd
    c["bs"] = cb[4096:4128].reshape(32, 1).copy()

    c["ss"] = np.ascontiguousarray(ins["L0_attn_score_scale"], np.float32).reshape(32, 1)
    c["slopes"] = np.concatenate([ins["L0_attn_decay_slopes"],
                                  ins["L0_attn_anchor_slopes"]]).astype(np.float32).reshape(32, 1)
    c["signm"] = np.array([1.0] * 28 + [-1.0] * 4, np.float32).reshape(32, 1)
    c["biasm"] = np.array([-float(L - 1)] * 28 + [0.0] * 4, np.float32).reshape(32, 1)

    c["mur"] = np.ascontiguousarray(ins["L0_ffn_mu_r"], np.float32).reshape(8, 128).T.copy()
    c["muk"] = np.ascontiguousarray(ins["L0_ffn_mu_k"], np.float32).reshape(8, 128).T.copy()
    c["dlog"] = np.ascontiguousarray(ins["L0_attn_deriv_logits"], np.float32).reshape(1, 3)

    nsv = np.ascontiguousarray(ins["L0_attn_norm_scale"], np.float32)
    c["ns2re"] = np.tile(nsv[:64], 2).reshape(128, 1)
    c["ns2im"] = np.tile(nsv[64:], 2).reshape(128, 1)

    wre = np.ascontiguousarray(ins["L0_attn_W_re"], np.float32)
    wim = np.ascontiguousarray(ins["L0_attn_W_im"], np.float32)
    bdre = np.zeros((128, 128), np.float32)
    bdre[:64, :64] = wre
    bdre[64:, 64:] = wre
    bdim = np.zeros((128, 128), np.float32)
    bdim[:64, :64] = wim
    bdim[64:, 64:] = wim
    c["bdre"] = bdre
    c["bdim"] = bdim

    sel = np.zeros((32, 16, 128), np.float32)
    for cc in range(16):
        sel[2 * cc, cc, :64] = 1.0
        sel[2 * cc + 1, cc, 64:] = 1.0
    c["sel"] = sel.reshape(32, 2048)

    selq = np.zeros((128, 16, 32), np.float32)
    for cc in range(16):
        selq[:64, cc, 2 * cc] = 1.0
        selq[64:, cc, 2 * cc + 1] = 1.0
    c["selq"] = selq.reshape(128, 512)

    c["ident"] = np.eye(128, dtype=np.float32)
    c["wr"] = np.ascontiguousarray(ins["L0_ffn_W_r"], np.float32)
    c["wk"] = np.ascontiguousarray(ins["L0_ffn_W_k"], np.float32)
    c["wv"] = np.ascontiguousarray(ins["L0_ffn_W_v"], np.float32)
    c["wout"] = np.ascontiguousarray(ins["L0_attn_out_proj_w"], np.float32)
    return c


def build_program():
    nc = bass.Bass("TRN2", target_bir_lowering=False)
    sc = Sched(nc)

    dx = nc.dram_tensor("x", [BSEQ, L, D], F32, kind="ExternalInput").ap() \
        .rearrange("s l d -> (s l) d")
    dout = nc.dram_tensor("out", [BSEQ, L, D], F32, kind="ExternalOutput").ap() \
        .rearrange("s l d -> (s l) d")
    dv = {}
    for name, shp in [
        ("ipw", [D, CT]), ("diag", [128, 16384]), ("diag_s", [32, 128]),
        ("theta_pc", [128, 16]), ("bdev", [128, 33]), ("bs", [32, 1]),
        ("ss", [32, 1]), ("slopes", [32, 1]), ("signm", [32, 1]),
        ("biasm", [32, 1]), ("mur", [128, 8]), ("muk", [128, 8]),
        ("dlog", [1, 3]), ("ns2re", [128, 1]), ("ns2im", [128, 1]),
        ("bdre", [128, 128]), ("bdim", [128, 128]), ("sel", [32, 2048]),
        ("selq", [128, 512]), ("ident", [128, 128]),
        ("wr", [D, D]), ("wk", [D, D]), ("wv", [D, D]), ("wout", [DI, D]),
    ]:
        dv[name] = nc.dram_tensor(name, shp, F32, kind="ExternalInput").ap()

    scr_ipw = nc.dram_tensor("scr_ipw", [33, 128, 1536], BF16, kind="Internal").ap()
    scr_wout = nc.dram_tensor("scr_wout", [16 * 128, D], BF16, kind="Internal").ap()
    scr_wr = nc.dram_tensor("scr_wr", [8, 128, D], BF16, kind="Internal").ap()
    scr_wk = nc.dram_tensor("scr_wk", [8, 128, D], BF16, kind="Internal").ap()
    scr_wv = nc.dram_tensor("scr_wv", [8, 128, D], BF16, kind="Internal").ap()

    A = nc.alloc_sbuf_tensor
    cmb_sb = A("cmb_sb", [128, 4 * 1536], BF16)
    diag_s_bf = A("diag_s_bf", [32, 128], BF16)
    sel_bf = A("sel_bf", [32, 2048], BF16)
    selq_bf = A("selq_bf", [128, 512], BF16)
    ident_bf = A("ident_bf", [128, 128], BF16)
    wre_bd = A("wre_bd", [128, 128], BF16)
    wim_bd = A("wim_bd", [128, 128], BF16)
    iota_c = A("iota_c", [32, TILE], F32)
    tw_sb = A("tw_sb", [32, TILE], F32)
    bias_t = A("bias_t", [32, 1], F32)
    theta_pc = A("theta_pc_sb", [128, 16], F32)
    thp = A("thp", [128, 16], F32)
    thpb = A("thpb", [128, 16], F32)
    thpb25 = A("thpb25", [128, 16], F32)
    bdev = A("bdev_sb", [128, 33], F32)
    bs_s = A("bs_s", [32, 1], F32)
    ss_s = A("ss_s", [32, 1], F32)
    slopes = A("slopes_sb", [32, 1], F32)
    signm = A("signm_sb", [32, 1], F32)
    biasm = A("biasm_sb", [32, 1], F32)
    spb = A("spb", [32, 1], F32)
    scale_v = A("scale_v", [32, 1], F32)
    bias_v = A("bias_v", [32, 1], F32)
    mur = A("mur_sb", [128, 8], F32)
    muk = A("muk_sb", [128, 8], F32)
    dlog_s = A("dlog_s", [1, 3], F32)
    e3 = A("e3", [1, 3], F32)
    ssum = A("ssum", [1, 1], F32)
    sinv = A("sinv", [1, 1], F32)
    wrow = A("wrow", [1, 3], F32)
    w_b = A("w_b", [128, 3], F32)
    nw2 = A("nw2", [128, 1], F32)
    ns2re = A("ns2re_sb", [128, 1], F32)
    ns2im = A("ns2im_sb", [128, 1], F32)
    bdstage = A("bdstage", [128, 128], F32)
    zeros32 = A("zeros32", [128, TILE], F32)
    wout_sb = A("wout_sb", [128, 2 * D], BF16)
    wrT_sb = A("wrT_sb", [128, 2 * D], BF16)
    wkT_sb = A("wkT_sb", [128, 2 * D], BF16)
    wvT_sb = A("wvT_sb", [128, 2 * D], BF16)
    x_sb = A("x_sb", [128, 4 * D], F32)
    out_sb = A("out_sb", [128, D], F32)
    xn_tm = A("xn_tm", [128, D], BF16)
    bnstat = A("bnstat", [128, 12 * NSUB], F32)
    bnout = A("bnout", [128, 2 * NSUB], F32)
    vv = A("vv", [128, NSUB], F32)
    rsv = A("rsv", [128, NSUB], F32)
    nsc1 = A("nsc1", [128, NSUB], F32)
    xn_fm = A("xn_fm", [128, NFC * XNW], BF16)
    z_ch = A("z_ch", [128, 2 * (TILE + 3)], BF16)
    z_hal = A("z_hal", [128, 33 * 3], BF16)
    z_s = A("z_s", [32, TILE + 3], BF16)
    num_re = A("num_re", [128, NKV * TILE], BF16)
    num_im = A("num_im", [128, NKV * TILE], BF16)
    sqre = A("sqre", [128, TILE], BF16)
    sqim = A("sqim", [128, TILE], BF16)
    bufA = A("bufA", [128, 2 * TILE], F32)
    bufB = A("bufB", [128, 2 * TILE], F32)
    bufC = A("bufC", [128, 3 * TILE], F32)
    bufD = A("bufD", [128, 2 * TILE], F32)
    bufE = A("bufE", [128, 2 * TILE], F32)
    bufG = A("bufG", [128, 2 * TILE], BF16)
    bufH = A("bufH", [128, 2 * TILE], BF16)
    bufF = A("bufF", [128, 3 * TILE], F32)
    rs_re = A("rs_re", [128, 2], F32)
    rs_im = A("rs_im", [128, 2], F32)
    carry_re = A("carry_re", [128, 16], F32)
    carry_im = A("carry_im", [128, 16], F32)
    pc_b = A("pc_b", [32, TILE], F32)
    p_b = A("p_b", [32, TILE], F32)
    pw_bf = A("pw_bf", [32, TILE], BF16)
    den_r = A("den_r", [32, TILE], F32)
    den_c = A("den_c", [32, 1], F32)
    g_f = A("g_f", [32, TILE], F32)
    g_bf = A("g_bf", [32, TILE], BF16)
    gate_b = A("gate_b", [128, NGA * TILE], BF16)
    mbuf = A("mbuf", [128, 2 * TILE], BF16)
    dbuf = A("dbuf", [128, TILE], BF16)
    xr_b = A("xr_b", [128, 2 * TILE], BF16)
    xk_b = A("xk_b", [128, 8 * TILE], BF16)
    kr_b = A("kr_b", [128, TILE], F32)
    kact = A("kact", [128, 8 * TILE], BF16)
    r_bf = A("r_bf", [128, 4 * D], BF16)

    ps = [nc.alloc_psum_tensor("ps%d" % i, [128, 512], F32) for i in range(8)]

    def dma(q, out_ap, in_ap, reads, writes, sem, first=True, last=True, accum=False):
        eng = "SY" if q == "SY" else "G"
        if accum:
            fn = lambda e, o=out_ap, i=in_ap: e.dma_start(o, i, accum_op=ALU.add)
        else:
            fn = lambda e, o=out_ap, i=in_ap: e.dma_start(o, i)
        return sc.emit(eng, fn, reads=reads, writes=writes, dma=sem,
                       first=first, last=last)

    def vop(fn, reads, writes, eng="V", first=True, last=True):
        return sc.emit(eng, fn, reads=reads, writes=writes, first=first, last=last)

    def mm(out_ap, lhsT_ap, rhs_ap, reads, psreg, start, stop):
        return sc.emit(
            "T",
            lambda e, o=out_ap, l=lhsT_ap, r=rhs_ap, st=start, sp=stop:
                e.matmul(o, l, r, start=st, stop=sp),
            reads=reads, writes=(psreg,), first=start, last=stop)

    def rsqrt_chain(dst, src, tmp, srcreg, dstreg, tmpreg):
        vop(lambda e: e.tensor_scalar(dst.bitcast(I32), src.bitcast(I32), 1, None,
                                      ALU.logical_shift_right),
            reads=(srcreg,), writes=(dstreg,))
        vop(lambda e: e.tensor_scalar(dst.bitcast(I32), dst.bitcast(I32), -1,
                                      RSQRT_MAGIC, ALU.mult, ALU.add),
            reads=(dstreg,), writes=(dstreg,))
        for _ in range(2):
            vop(lambda e: e.scalar_tensor_tensor(tmp, dst, 1.0, dst, ALU.mult, ALU.mult),
                reads=(dstreg,), writes=(tmpreg,))
            vop(lambda e: e.scalar_tensor_tensor(tmp, tmp, -0.5, src, ALU.mult, ALU.mult),
                reads=(tmpreg, srcreg), writes=(tmpreg,))
            vop(lambda e: e.tensor_scalar(tmp, tmp, 1.5, None, ALU.add),
                reads=(tmpreg,), writes=(tmpreg,))
            vop(lambda e: e.scalar_tensor_tensor(dst, dst, 1.0, tmp, ALU.mult, ALU.mult),
                reads=(dstreg, tmpreg), writes=(dstreg,))

    tap_outs = {}

    def tap(name, ap, regs):
        if TAPS is None or name not in TAPS or name in tap_outs:
            return
        shp = list(ap.shape)
        dt_ = ap.dtype
        t = nc.dram_tensor("dbg_" + name, shp, dt_, kind="ExternalOutput").ap()
        tap_outs[name] = t
        dma("SY", t, ap, tuple(regs), ("dbg" + name,), "dO")

    for chp in range(33):
        wq = 128 if chp < 32 else 32
        dma("G", scr_ipw[chp][:, 0:1024].rearrange("p (k q) -> p k q", k=8)[:, :, 0:wq],
            dv["ipw"].rearrange("(k p) c -> p k c", p=128)
            [:, :, chp * 128: chp * 128 + wq],
            (), ("scripw",), "dG", first=False, last=(chp == 32))
    for chp in range(32):
        dma("G", scr_ipw[chp][:, 1024:1536], dv["diag"][:, chp * 512:(chp + 1) * 512],
            (), ("scripw",), "dG", first=(chp == 0), last=False)
    dma("G", diag_s_bf[:], dv["diag_s"], (), ("diag_s",), "dG")
    dma("G", sel_bf[:], dv["sel"], (), ("sel",), "dG")
    dma("G", selq_bf[:], dv["selq"], (), ("selq",), "dG")
    dma("G", ident_bf[:], dv["ident"], (), ("ident",), "dG")
    dma("G", scr_wout, dv["wout"], (), ("scrwout",), "dG")
    for nm, dst in [("theta_pc", theta_pc), ("bdev", bdev), ("bs", bs_s),
                    ("ss", ss_s), ("slopes", slopes), ("signm", signm),
                    ("biasm", biasm), ("mur", mur), ("muk", muk),
                    ("dlog", dlog_s), ("ns2re", ns2re), ("ns2im", ns2im)]:
        dma("SY", dst[:], dv[nm], (), (nm,), "dX")
    dma("SY", bdstage[:], dv["bdre"], (), ("bdstage",), "dX")

    vop(lambda e: e.memset(zeros32[:], 0.0), reads=(), writes=("zeros",))
    vop(lambda e: e.tensor_scalar(thp[:], theta_pc[:], 1.0 / TWO_PI, None, ALU.mult),
        reads=("theta_pc",), writes=("thp",))
    vop(lambda e: e.tensor_mul(thpb[:], thp[:], bdev[:, 0:16]),
        reads=("thp", "bdev"), writes=("thpb",))
    vop(lambda e: e.tensor_scalar(thpb25[:], thpb[:], 0.25, None, ALU.add),
        reads=("thpb",), writes=("thpb25",))
    vop(lambda e: e.tensor_scalar(wre_bd[:], bdstage[:], ns2re[:], -1.0, ALU.mult, ALU.mult),
        reads=("bdstage", "ns2re"), writes=("wrebd",))
    dma("SY", bdstage[:], dv["bdim"], ("bdstage",), ("bdstage",), "dX")
    vop(lambda e: e.tensor_scalar(wim_bd[:], bdstage[:], ns2im[:], -1.0, ALU.mult, ALU.mult),
        reads=("bdstage", "ns2im"), writes=("wimbd",))

    vop(lambda e: e.activation(spb[:], slopes[:], AF.Exp, bias=0.0, scale=1.0),
        reads=("slopes",), writes=("spb",), eng="S")
    vop(lambda e: e.tensor_scalar(spb[:], spb[:], 1.0, None, ALU.add),
        reads=("spb",), writes=("spb",))
    vop(lambda e: e.activation(spb[:], spb[:], AF.Ln, bias=0.0, scale=1.0),
        reads=("spb",), writes=("spb",), eng="S")
    vop(lambda e: e.tensor_mul(scale_v[:], spb[:], signm[:]),
        reads=("spb", "signm"), writes=("scalev",))
    vop(lambda e: e.tensor_mul(bias_v[:], spb[:], biasm[:]),
        reads=("spb", "biasm"), writes=("biasv",))
    vop(lambda e: e.iota(iota_c[:], [[1, TILE]], channel_multiplier=0,
                         allow_small_or_imprecise_dtypes=True),
        reads=(), writes=("iotac",), eng="G")
    vop(lambda e: e.activation(e3[:], dlog_s[:], AF.Exp, bias=0.0, scale=1.0),
        reads=("dlog",), writes=("e3",), eng="S")
    vop(lambda e: e.reduce_sum(ssum[:], e3[:], axis=mybir.AxisListType.X),
        reads=("e3",), writes=("ssum",))
    vop(lambda e: e.reciprocal(sinv[:], ssum[:]), reads=("ssum",), writes=("sinv",))
    vop(lambda e: e.tensor_scalar(wrow[:], e3[:], sinv[:], None, ALU.mult),
        reads=("e3", "sinv"), writes=("wrow",))
    dma("SY", w_b[0:1, :], wrow[:], ("wrow",), ("w_b",), "dX")
    kk2 = 1
    while kk2 < 128:
        dma("SY", w_b[kk2:2 * kk2, :], w_b[0:kk2, :], ("w_b",), ("w_b",), "dX")
        kk2 *= 2
    vop(lambda e: e.tensor_scalar(nw2[:], w_b[:, 2:3], -1.0, None, ALU.mult),
        reads=("w_b",), writes=("nw2",))

    for wname, src, scr in (("wr", dv["wr"], scr_wr), ("wk", dv["wk"], scr_wk),
                            ("wv", dv["wv"], scr_wv)):
        for kk in range(8):
            dma("G", kact[:, 1024:2048], src[kk * 128:(kk + 1) * 128, :],
                ("kact4", "kact5", "kact6", "kact7"),
                ("kact4", "kact5", "kact6", "kact7"), "dG")
            for t8 in range(8):
                bank = t8 // 4
                mm(ps[bank][:, (t8 % 4) * 128:(t8 % 4 + 1) * 128],
                   kact[:, 1024 + t8 * 128:1024 + (t8 + 1) * 128], ident_bf[:],
                   reads=("ident", "kact4", "kact5", "kact6", "kact7"),
                   psreg="pb%d" % bank, start=True, stop=True)
            for bank in range(2):
                vop(lambda e, b=bank: e.tensor_copy(
                    kact[:, b * 512:(b + 1) * 512], ps[b][:]),
                    reads=("pb%d" % bank,),
                    writes=("kact%d" % (2 * bank), "kact%d" % (2 * bank + 1)))
            if wname == "wk":
                dma("SY", scr.rearrange("k p d -> (k p) d")[kk * 128:(kk + 1) * 128, :],
                    kact[:, 0:1024],
                    ("kact0", "kact1", "kact2", "kact3"), ("scr" + wname,), "dX",
                    first=(kk == 0), last=(kk == 7))
            else:
                dma("SY",
                    scr[:, :, kk * 128:(kk + 1) * 128].transpose([1, 0, 2]),
                    kact[:, 0:1024].rearrange("p (m w) -> p m w", w=128),
                    ("kact0", "kact1", "kact2", "kact3"), ("scr" + wname,), "dX",
                    first=(kk == 0), last=(kk == 7))

    def xslot(gsub):
        return gsub % 4

    def x_load(tt):
        seq, t = divmod(tt, NT)
        for sub in range(NSUB):
            sl = xslot(tt * NSUB + sub)
            row = seq * L + t * TILE + sub * 128
            dma("SY", x_sb[:, sl * D:(sl + 1) * D], dx[row:row + 128, :],
                (), ("x%d" % sl,), "dX")

    def xn_chunk(kk, a, b):
        return xn_fm[:, kk * XNW + a: kk * XNW + b]

    scrw_wout = scr_wout

    def stream(buf, name, piece_ap, idx, q="SY"):
        par = idx % 2
        reg = "%s%d" % (name, par)
        dma(q, buf[:, par * D:(par + 1) * D], piece_ap, ("scr" + name,), (reg,), "dX")
        return reg, buf[:, par * D:(par + 1) * D]

    x_load(0)

    for tt in range(NTT):
        seq, t = divmod(tt, NT)
        t0 = t * TILE

        def ln_block(tt2):
            if tt2 % NT == 0:
                vop(lambda e: e.memset(
                    xn_fm[:].rearrange("p (c w) -> p c w", w=XNW)[:, :, 0:4], 0.0),
                    reads=(), writes=("xn",))
                vop(lambda e: e.memset(carry_re[:], 0.0), reads=(), writes=("c_re",))
                vop(lambda e: e.memset(carry_im[:], 0.0), reads=(), writes=("c_im",))
                vop(lambda e: e.memset(den_c[:], 0.0), reads=(), writes=("den_c",))
                vop(lambda e: e.memset(z_hal[:], 0.0), reads=(), writes=("zhal",))
            else:
                vop(lambda e: e.tensor_copy(
                    xn_fm[:].rearrange("p (c w) -> p c w", w=XNW)[:, :, 0:4],
                    xn_fm[:].rearrange("p (c w) -> p c w", w=XNW)[:, :, TILE:TILE + 4]),
                    reads=("xn",), writes=("xn",))

            if True:
                for sub in range(NSUB):
                    sl = xslot(tt2 * NSUB + sub)
                    xap = x_sb[:, sl * D:(sl + 1) * D]
                    for hh in range(2):
                        vop(lambda e, xa=xap, s=sub, h=hh: e.bn_stats(
                            bnstat[:, s * 12 + h * 6: s * 12 + (h + 1) * 6],
                            xa[:, h * 512:(h + 1) * 512]),
                            reads=("x%d" % sl,), writes=("bnst%d" % sub,),
                            first=(hh == 0), last=(hh == 1))
                    vop(lambda e, s=sub: e.bn_aggr(bnout[:, s * 2:(s + 1) * 2],
                                                   bnstat[:, s * 12:(s + 1) * 12]),
                        reads=("bnst%d" % sub,), writes=("bno%d" % sub,))
                    vop(lambda e, s=sub: e.tensor_copy(vv[:, s:s + 1],
                                                       bnout[:, s * 2 + 1:s * 2 + 2]),
                        reads=("bno%d" % sub,), writes=("vv",), first=(sub == 0),
                        last=(sub == NSUB - 1))
                vop(lambda e: e.tensor_scalar(vv[:], vv[:], 1e-5, None, ALU.add),
                    reads=("vv",), writes=("vv",))
                rsqrt_chain(rsv[:], vv[:], nsc1[:], "vv", "rsv", "lnt")
                for sub in range(NSUB):
                    sl = xslot(tt2 * NSUB + sub)
                    xap = x_sb[:, sl * D:(sl + 1) * D]
                    vop(lambda e, xa=xap, s=sub: e.tensor_scalar(
                        xn_tm[:], xa, bnout[:, s * 2:s * 2 + 1], rsv[:, s:s + 1],
                        ALU.subtract, ALU.mult),
                        reads=("x%d" % sl, "bno%d" % sub, "rsv"), writes=("xntm",))
                    for t8 in range(8):
                        bank = t8 // 4
                        mm(ps[bank][:, (t8 % 4) * 128:(t8 % 4 + 1) * 128],
                           xn_tm[:, t8 * 128:(t8 + 1) * 128], ident_bf[:],
                           reads=("xntm", "ident"), psreg="pb%d" % bank,
                           start=True, stop=True)
                    for bank in range(2):
                        cc0 = bank * 4
                        vop(lambda e, b=bank, c0=cc0, s=sub: e.tensor_copy(
                            xn_fm[:].rearrange("p (c w) -> p c w", w=XNW)
                            [:, c0:c0 + 4, 4 + s * 128:4 + (s + 1) * 128],
                            ps[b][:].rearrange("p (c w) -> p c w", w=128)),
                            reads=("pb%d" % bank,), writes=("xn",))

                if tt == 0:
                    tap("xn", xn_fm[:], ("xn",))
                    tap("xld", x_sb[:, 0:D], ("x0",))
                    tap("bno", bnout[:], ("bno0", "bno1"))
                    tap("rsv", rsv[:], ("rsv",))
                    tap("bnst", bnstat[:], ("bnst0", "bnst1"))
                    tap("vv", vv[:], ("vv",))
                    tap("xntm", xn_tm[:], ("xntm",))


        if tt == 0:
            ln_block(0)

        ipw_seq = [32] + [p // 2 if p % 2 == 0 else 16 + p // 2 for p in range(32)]

        def ipw_stream(si_):
            ch = ipw_seq[si_]
            sl4 = si_ % 4
            dma("SY", cmb_sb[:, sl4 * 1536:(sl4 + 1) * 1536], scr_ipw[ch],
                ("scripw",), ("ipwp%d" % sl4,), "dX")

        def ipw_piece(si_):
            sl4 = si_ % 4
            return "ipwp%d" % sl4, cmb_sb[:, sl4 * 1536: sl4 * 1536 + 1024]

        def diag_piece(p):
            si_ = p + 1
            sl4 = si_ % 4
            return "ipwp%d" % sl4, cmb_sb[:, sl4 * 1536 + 1024:(sl4 + 1) * 1536]

        def sb_block(tt2):
            if True:
                if PH2 > 0:
                    for si_ in range(4):
                        ipw_stream(si_)

                if PH2 > 1:
                    sreg0, sap0 = ipw_piece(0)
                    for kk in range(8):
                        mm(ps[6][0:32, 0:TILE], sap0[:, kk * 128: kk * 128 + 32],
                           xn_chunk(kk, 4, XNW), reads=(sreg0, "xn"), psreg="pb6",
                           start=(kk == 0), stop=(kk == 7))

                if PH2 > 2:
                    vop(lambda e: e.tensor_copy(z_s[:, 0:3], z_hal[0:32, 96:99]),
                        reads=("zhal",), writes=("zhl_s",))
                    vop(lambda e: e.activation(z_s[:, 3:TILE + 3], ps[6][0:32, 0:TILE], AF.Copy,
                                               bias=0.0, scale=1.0),
                        reads=("pb6",), writes=("z_s",), eng="S")
                    vop(lambda e: e.activation(z_hal[0:32, 96:99], ps[6][0:32, TILE - 3:TILE],
                                               AF.Copy, bias=0.0, scale=1.0),
                        reads=("pb6",), writes=("zhal",), eng="S")

                if PH2 > 3:
                    for j in range(4):
                        mm(ps[7][0:32, 0:TILE], diag_s_bf[:, j * 32:(j + 1) * 32],
                           z_s[:, j:j + TILE], reads=("diag_s", "z_s", "zhl_s"), psreg="pb7",
                           start=(j == 0), stop=(j == 3))

                if PH2 > 4:
                    vop(lambda e: e.tensor_scalar(pc_b[:], ps[7][0:32, 0:TILE], bs_s[:], ss_s[:],
                                                  ALU.add, ALU.mult),
                        reads=("pb7", "bs", "ss"), writes=("pc",))
                    vop(lambda e: e.tensor_scalar(pc_b[:], pc_b[:], 20.0, -20.0, ALU.min, ALU.max),
                        reads=("pc",), writes=("pc",))

                if PH2 > 5:
                    vop(lambda e, tq=float((tt2 % NT) * TILE): e.scalar_tensor_tensor(
                        bias_t[:], scale_v[:], tq, bias_v[:], ALU.mult, ALU.add),
                        reads=("scalev", "biasv"), writes=("biast",))
                    vop(lambda e: e.activation(p_b[:], pc_b[:], AF.Exp, bias=0.0, scale=1.0),
                        reads=("pc",), writes=("p_b",), eng="S")
                    vop(lambda e: e.activation(tw_sb[:], iota_c[:], AF.Exp, bias=bias_t[:],
                                               scale=scale_v[:]),
                        reads=("iotac", "biast", "scalev"), writes=("tw",), eng="S")

                if PH2 > 6:
                    vop(lambda e: e.tensor_mul(pw_bf[:], p_b[:], tw_sb[:]),
                        reads=("p_b", "tw"), writes=("pw",))

                if PH2 > 7:
                    vop(lambda e: e.tensor_tensor_scan(den_r[:], zeros32[0:32, :], pw_bf[:],
                                                       den_c[:], ALU.add, ALU.add),
                        reads=("zeros", "pw", "den_c"), writes=("den",))
                    vop(lambda e: e.tensor_copy(den_c[:], den_r[:, TILE - 1:TILE]),
                        reads=("den",), writes=("den_c",))
                    vop(lambda e: e.tensor_scalar(den_r[:], den_r[:], 1e-4, None, ALU.max),
                        reads=("den",), writes=("den",))

                    if tt == 0:
                        tap("zs", z_s[:], ("z_s", "zhl_s"))
                        tap("tw", tw_sb[:], ("tw",))
                        tap("p", p_b[:], ("p_b",))
                        tap("pw", pw_bf[:], ("pw",))
                        tap("den", den_r[:], ("den",))



        if tt == 0:
            sb_block(0)

        if PH > 2:
            for c in range(NFC):
                db = dbuf[:, 0:TILE]
                vop(lambda e, o=db, cc=c: e.tensor_sub(o, xn_chunk(cc, 4, XNW),
                                                       xn_chunk(cc, 3, XNW - 1)),
                    reads=("xn",), writes=("db",))
                vop(lambda e, i=db, cc=c: e.scalar_tensor_tensor(
                    xk_b[:, cc * TILE:(cc + 1) * TILE], i, muk[:, cc:cc + 1],
                    xn_chunk(cc, 3, XNW - 1), ALU.mult, ALU.add),
                    reads=("db", "muk", "xn"), writes=("xkq%d" % c,))

            def k_group(m):
                sreg, wap = stream(wkT_sb, "wk",
                                   scr_wk.rearrange("k p d -> (k p) d")
                                   [m * 128:(m + 1) * 128, :], tt * 8 + m, q="G")
                for c in range(NFC):
                    mm(ps[4][:, 0:TILE], wap[:, c * 128:(c + 1) * 128],
                       xk_b[:, c * TILE:(c + 1) * TILE],
                       reads=(sreg, "xkq%d" % c), psreg="pb4",
                       start=(c == 0), stop=(c == NFC - 1))
                vop(lambda e: e.tensor_scalar(kr_b[:, 0:TILE], ps[4][:, 0:TILE], 0.0,
                                              None, ALU.max),
                    reads=("pb4",), writes=("kr",))
                vop(lambda e, mq=m: e.activation(
                    kact[:, mq * TILE:(mq + 1) * TILE], kr_b[:, 0:TILE], AF.Square,
                    bias=0.0, scale=1.0),
                    reads=("kr",), writes=("kact%d" % m,), eng="S")


        if PH > 3:
            def pa_ctx(p):
                c = p // 2
                kind = p % 2
                pi = p % 2
                cpi = c % 2
                p3 = c % 3
                d = dict(
                    p=p, c=c, kind=kind, pi=pi, cpi=cpi,
                    zb=ps[pi], zreg="pb%d" % pi, cb=ps[2 + pi],
                    creg="pb%d" % (2 + pi),
                    zc=z_ch[:, pi * (TILE + 3):(pi + 1) * (TILE + 3)])
                if kind == 0:
                    d.update(
                        pwb=ps[6 + cpi], pwreg="pb%d" % (6 + cpi),
                        bA=bufA[:, cpi * TILE:(cpi + 1) * TILE],
                        bB=bufB[:, cpi * TILE:(cpi + 1) * TILE],
                        bC=bufC[:, p3 * TILE:(p3 + 1) * TILE],
                        bD=bufD[:, cpi * TILE:(cpi + 1) * TILE],
                        bE=bufE[:, cpi * TILE:(cpi + 1) * TILE],
                        bF=bufF[:, p3 * TILE:(p3 + 1) * TILE],
                        bG=bufG[:, cpi * TILE:(cpi + 1) * TILE],
                        bH=bufH[:, cpi * TILE:(cpi + 1) * TILE],
                        sqr=sqre[:, 0:TILE], sqi=sqim[:, 0:TILE],
                        rA="bA%d" % cpi, rB="bB%d" % cpi, rC="bC%d" % p3,
                        rD="bD%d" % cpi, rE="bE%d" % cpi, rF="bF%d" % p3,
                        rG="bG%d" % cpi, rH="bH%d" % cpi)
                return d

            def paT_inproj(k):
                p = k["p"]
                sreg, sap = ipw_piece(p + 1)
                for kk in range(8):
                    mm(k["zb"][:, 0:TILE], sap[:, kk * 128:(kk + 1) * 128],
                       xn_chunk(kk, 4, XNW), reads=(sreg, "xn"), psreg=k["zreg"],
                       start=(kk == 0), stop=(kk == 7))

            def paS_zevac(k):
                ch = k["c"] if k["kind"] == 0 else 16 + k["c"]
                vop(lambda e, z=k["zc"], c3=ch: e.tensor_copy(z[:, 0:3],
                                                              z_hal[:, c3 * 3:(c3 + 1) * 3]),
                    reads=("zhal",), writes=("zhl%d" % k["pi"],))
                vop(lambda e, z=k["zc"], b=k["zb"]: e.activation(
                    z[:, 3:TILE + 3], b[:, 0:TILE], AF.Copy, bias=0.0, scale=1.0),
                    reads=(k["zreg"],), writes=("zch%d" % k["pi"],), eng="S")
                vop(lambda e, b=k["zb"], c3=ch: e.activation(
                    z_hal[:, c3 * 3:(c3 + 1) * 3], b[:, TILE - 3:TILE],
                    AF.Copy, bias=0.0, scale=1.0),
                    reads=(k["zreg"],), writes=("zhal",), eng="S")
                if tt == 0 and k["p"] == 0:
                    tap("z0", k["zc"], ("zch0", "zhl0"))

            def paT_conv(k):
                dreg, dap = diag_piece(k["p"])
                for j in range(4):
                    mm(k["cb"][:, 0:TILE], dap[:, j * 128:(j + 1) * 128],
                       k["zc"][:, j:j + TILE],
                       reads=(dreg, "zch%d" % k["pi"], "zhl%d" % k["pi"]),
                       psreg=k["creg"], start=(j == 0), stop=(j == 3))
                if k["p"] + 4 < 33:
                    ipw_stream(k["p"] + 4)

            def paS_extract(k):
                if k["kind"] == 1:
                    c = k["c"]
                    vop(lambda e, b=k["cb"], cc=16 + c: e.activation(
                        gate_b[:, c * TILE:(c + 1) * TILE], b[:, 0:TILE], AF.Silu,
                        bias=bdev[:, cc:cc + 1], scale=1.0),
                        reads=(k["creg"], "bdev"), writes=("gatef%d" % c,), eng="S")
                    return
                c = k["c"]
                vop(lambda e, b=k["cb"], o=k["bA"], cc=c: e.activation(
                    o, b[:, 0:TILE], AF.Identity, bias=thpb[:, cc:cc + 1],
                    scale=thp[:, cc:cc + 1]),
                    reads=(k["creg"], "thpb", "thp"), writes=(k["rA"],), eng="S")
                vop(lambda e, b=k["cb"], o=k["bB"], cc=c: e.activation(
                    o, b[:, 0:TILE], AF.Identity, bias=thpb25[:, cc:cc + 1],
                    scale=thp[:, cc:cc + 1]),
                    reads=(k["creg"], "thpb25", "thp"), writes=(k["rB"],), eng="S")
                vop(lambda e, b=k["cb"], o=k["bC"], cc=c: e.activation(
                    o, b[:, 0:TILE], AF.Identity, bias=bdev[:, cc:cc + 1], scale=1.0),
                    reads=(k["creg"], "bdev"), writes=(k["rC"],), eng="S")
                if tt == 0 and k["p"] == 0:
                    tap("xv0", k["bC"], (k["rC"],))
                vop(lambda e, i=k["bC"], o=k["bF"]: e.activation(o, i, AF.Square,
                                                                 bias=0.0, scale=1.0),
                    reads=(k["rC"],), writes=(k["rF"],), eng="S")
                vop(lambda e, i=k["bC"]: e.activation(i, i, AF.Identity,
                                                      bias=w_b[:, 0:1], scale=w_b[:, 1:2]),
                    reads=(k["rC"], "w_b"), writes=(k["rC"],), eng="S")

            def paVT_magic(k):
                if k["kind"] == 1:
                    return
                c = k["c"]
                vop(lambda e, i=k["bA"], o=k["bD"]: e.activation(o, i, AF.Copy,
                                                                 bias=MAGIC, scale=1.0),
                    reads=(k["rA"],), writes=(k["rD"],), eng="S")
                vop(lambda e, i=k["bB"], o=k["bE"]: e.activation(o, i, AF.Copy,
                                                                 bias=MAGIC, scale=1.0),
                    reads=(k["rB"],), writes=(k["rE"],), eng="S")
                vop(lambda e, i=k["bD"], r=k["bA"]: e.scalar_tensor_tensor(
                    i, i, MAGIC, r, ALU.subtract, ALU.subtract),
                    reads=(k["rD"], k["rA"]), writes=(k["rD"],))
                vop(lambda e, i=k["bE"], r=k["bB"]: e.scalar_tensor_tensor(
                    i, i, MAGIC, r, ALU.subtract, ALU.subtract),
                    reads=(k["rE"], k["rB"]), writes=(k["rE"],))
                mm(k["pwb"][:, 0:TILE], sel_bf[:, c * 128:(c + 1) * 128], pw_bf[:],
                   reads=("sel", "pw"), psreg=k["pwreg"], start=True, stop=True)

            def paS_trig(k):
                if k["kind"] == 1:
                    return
                vop(lambda e, i=k["bD"]: e.activation(i, i, AF.Sin, bias=0.0, scale=TWO_PI),
                    reads=(k["rD"],), writes=(k["rD"],), eng="S")
                vop(lambda e, i=k["bE"]: e.activation(i, i, AF.Sin, bias=0.0, scale=TWO_PI),
                    reads=(k["rE"],), writes=(k["rE"],), eng="S")

            def paV_poly(k):
                if k["kind"] == 1:
                    return
                c = k["c"]
                vop(lambda e, i=k["bF"], t1=k["bC"]: e.scalar_tensor_tensor(
                    i, i, nw2[:], t1, ALU.mult, ALU.add),
                    reads=(k["rF"], "nw2", k["rC"]), writes=(k["rF"],))
                if tt == 0 and c == 0:
                    tap("sin0", k["bD"], (k["rD"],))
                    tap("t10", k["bC"], (k["rC"],))
                vop(lambda e, i=k["bF"], b=k["pwb"]: e.tensor_mul(i, i, b[:, 0:TILE]),
                    reads=(k["rF"], k["pwreg"]), writes=(k["rF"],))
                vop(lambda e, i=k["bF"], cn=k["bE"], o=k["bG"], p=k["cpi"]: e.scalar_tensor_tensor(
                    o, i, 1.0, cn, ALU.mult, ALU.mult, accum_out=rs_re[:, p:p + 1]),
                    reads=(k["rF"], k["rE"]), writes=(k["rG"], "rsre%d" % k["cpi"]))
                vop(lambda e, i=k["bF"], sn=k["bD"], o=k["bH"], p=k["cpi"]: e.scalar_tensor_tensor(
                    o, i, 1.0, sn, ALU.mult, ALU.mult, accum_out=rs_im[:, p:p + 1]),
                    reads=(k["rF"], k["rD"]), writes=(k["rH"], "rsim%d" % k["cpi"]))

            def paV_scan(k):
                if k["kind"] == 1:
                    return
                c = k["c"]
                vop(lambda e, a=k["bG"], cc=c: e.tensor_tensor_scan(
                    num_re[:, cc * TILE:(cc + 1) * TILE], zeros32[:], a,
                    carry_re[:, cc:cc + 1], ALU.add, ALU.add),
                    reads=("zeros", k["rG"], "c_re"), writes=("numre%d" % c,))
                vop(lambda e, a=k["bH"], cc=c: e.tensor_tensor_scan(
                    num_im[:, cc * TILE:(cc + 1) * TILE], zeros32[:], a,
                    carry_im[:, cc:cc + 1], ALU.add, ALU.add),
                    reads=("zeros", k["rH"], "c_im"), writes=("numim%d" % c,))
                vop(lambda e, cc=c, p=k["cpi"]: e.tensor_add(
                    carry_re[:, cc:cc + 1], carry_re[:, cc:cc + 1], rs_re[:, p:p + 1]),
                    reads=("c_re", "rsre%d" % k["cpi"]), writes=("c_re",))
                vop(lambda e, cc=c, p=k["cpi"]: e.tensor_add(
                    carry_im[:, cc:cc + 1], carry_im[:, cc:cc + 1], rs_im[:, p:p + 1]),
                    reads=("c_im", "rsim%d" % k["cpi"]), writes=("c_im",))

            def paS_sq(k):
                if k["kind"] == 1:
                    return
                c = k["c"]
                vop(lambda e, cc=c, o=k["sqr"]: e.activation(
                    o, num_re[:, cc * TILE:(cc + 1) * TILE], AF.Square, bias=0.0, scale=1.0),
                    reads=("numre%d" % c,), writes=("sqre",), eng="S")
                vop(lambda e, cc=c, o=k["sqi"]: e.activation(
                    o, num_im[:, cc * TILE:(cc + 1) * TILE], AF.Square, bias=0.0, scale=1.0),
                    reads=("numim%d" % c,), writes=("sqim",), eng="S")

            def paT_ssel(k):
                if k["kind"] == 1:
                    return
                c = k["c"]
                mm(ps[5][0:32, 0:TILE], selq_bf[:, c * 32:(c + 1) * 32], k["sqr"],
                   reads=("selq", "sqre"), psreg="pb5",
                   start=(c == 0), stop=False)
                mm(ps[5][0:32, 0:TILE], selq_bf[:, c * 32:(c + 1) * 32], k["sqi"],
                   reads=("selq", "sqim"), psreg="pb5",
                   start=False, stop=(c == NKV - 1))

            pa_stages = [paT_inproj, paS_zevac, paT_conv, paS_extract, paVT_magic,
                         paS_trig, paV_poly, paV_scan, paS_sq, paT_ssel]
            NP = 32
            ctxs = {p: pa_ctx(p) for p in range(NP)}
            ns_ = len(pa_stages)
            for i in range(NP + ns_ - 1):
                for si in range(ns_ - 1, -1, -1):
                    p = i - si
                    if 0 <= p < NP:
                        pa_stages[si](ctxs[p])
                if i >= 6 and (i - 6) % 4 == 0 and (i - 6) // 4 < 8:
                    k_group((i - 6) // 4)


        if PH > 4:
            vop(lambda e: e.scalar_tensor_tensor(pc_b[:], den_r[:], 1e-5, den_r[:],
                                                 ALU.mult, ALU.mult),
                reads=("den",), writes=("pc",))
            vop(lambda e: e.scalar_tensor_tensor(p_b[:], ps[5][0:32, 0:TILE],
                                                 1.0 / 128.0, pc_b[:], ALU.mult, ALU.add),
                reads=("pb5", "pc"), writes=("p_b",))
            rsqrt_chain(g_f[:], p_b[:], bufF[0:32, 0:TILE], "p_b", "g_f", "bF0")
            vop(lambda e: e.tensor_copy(g_bf[:], g_f[:]), reads=("g_f",), writes=("g_bf",))

            if tt == 0:
                tap("numre", num_re[:], tuple("numre%d" % i for i in range(16)))
                tap("numim", num_im[:], tuple("numim%d" % i for i in range(16)))
                tap("g", g_f[:], ("g_f",))

        if PH > 5:
            def b_ctx(c):
                cpi = c % 2
                return dict(c=c, cpi=cpi, yb=ps[2 + cpi], yreg="pb%d" % (2 + cpi),
                            gb=ps[4 + cpi], greg="pb%d" % (4 + cpi),
                            mb=mbuf[:, cpi * TILE:(cpi + 1) * TILE])

            def bT_wp(k):
                c = k["c"]
                mm(k["yb"][:, 0:TILE], wre_bd[:], num_re[:, c * TILE:(c + 1) * TILE],
                   reads=("wrebd", "numre%d" % c), psreg=k["yreg"], start=True, stop=False)
                mm(k["yb"][:, 0:TILE], wim_bd[:], num_im[:, c * TILE:(c + 1) * TILE],
                   reads=("wimbd", "numim%d" % c), psreg=k["yreg"], start=False, stop=True)
                mm(k["gb"][:, 0:TILE], sel_bf[:, c * 128:(c + 1) * 128], g_bf[:],
                   reads=("sel", "g_bf"), psreg=k["greg"], start=True, stop=True)

            def bV_yg(k):
                c = k["c"]
                vop(lambda e, m=k["mb"], b=k["gb"], cc=c: e.tensor_mul(
                    m, gate_b[:, cc * TILE:(cc + 1) * TILE], b[:, 0:TILE]),
                    reads=("gatef%d" % c, k["greg"]), writes=("mb%d" % k["cpi"],))
                vop(lambda e, m=k["mb"], b=k["yb"], cc=c: e.tensor_mul(
                    num_re[:, cc * TILE:(cc + 1) * TILE], b[:, 0:TILE], m),
                    reads=("mb%d" % k["cpi"], k["yreg"], "numre%d" % c), writes=("numre%d" % c,))

            ATTNB = [0, 1, 6, 7]

            b_stages = [bT_wp, bV_yg]
            bctxs = {c: b_ctx(c) for c in range(NGA)}
            nb_ = len(b_stages)
            for i in range(NGA + nb_ - 1):
                for si in range(nb_ - 1, -1, -1):
                    c = i - si
                    if 0 <= c < NGA:
                        b_stages[si](bctxs[c])


        if PH > 6:
            for c in range(NGA):
                sreg, wap = stream(wout_sb, "wout", scrw_wout[c * 128:(c + 1) * 128, :],
                                   tt * 16 + c, q="G")
                for sub in range(NSUB):
                    for half in range(2):
                        bank = sub * 2 + half
                        mm(ps[bank][:, 0:512],
                           num_re[:, c * TILE + sub * 128: c * TILE + (sub + 1) * 128],
                           wap[:, half * 512:(half + 1) * 512],
                           reads=("numre%d" % c, sreg), psreg="pb%d" % bank,
                           start=(c == 0), stop=(c == NGA - 1))
            for sub in range(NSUB):
                sl = xslot(tt * NSUB + sub)
                for half in range(2):
                    bank = sub * 2 + half
                    vop(lambda e, b=bank, s=sl, h=half: e.scalar_tensor_tensor(
                        x_sb[:, s * D + h * 512: s * D + (h + 1) * 512],
                        ps[b][:, 0:512], 1.0,
                        x_sb[:, s * D + h * 512: s * D + (h + 1) * 512],
                        ALU.mult, ALU.add),
                        reads=("pb%d" % bank, "x%d" % sl), writes=("x%d" % sl,))

            for sub in range(NSUB):
                sl = xslot(tt * NSUB + sub)
                row = seq * L + t0 + sub * 128
                dma("SY", dout[row:row + 128, :], x_sb[:, sl * D:(sl + 1) * D],
                    ("x%d" % sl,), ("or%d_%d" % (tt, sub),), "dO")
            if tt + 1 < NTT:
                x_load(tt + 1)

        if PH > 7:
            def c_r(c):
                pi = c % 2
                db = dbuf[:, 0:TILE]
                xrp = xr_b[:, pi * TILE:(pi + 1) * TILE]
                sreg, wap = stream(wrT_sb, "wr",
                                   scr_wr.rearrange("m p d -> (m p) d")
                                   [c * 128:(c + 1) * 128, :], tt * 8 + c, q="G")
                vop(lambda e, o=db, cc=c: e.tensor_sub(o, xn_chunk(cc, 4, XNW),
                                                       xn_chunk(cc, 3, XNW - 1)),
                    reads=("xn",), writes=("db",))
                vop(lambda e, i=db, o=xrp, cc=c: e.scalar_tensor_tensor(
                    o, i, mur[:, cc:cc + 1], xn_chunk(cc, 3, XNW - 1), ALU.mult, ALU.add),
                    reads=("db", "mur", "xn"), writes=("xr%d" % pi,))
                for sub in range(NSUB):
                    for half in range(2):
                        bank = sub * 2 + half
                        mm(ps[bank][:, 0:512], xrp[:, sub * 128:(sub + 1) * 128],
                           wap[:, half * 512:(half + 1) * 512],
                           reads=("xr%d" % pi, sreg), psreg="pb%d" % bank,
                           start=(c == 0), stop=(c == NFC - 1))

            for c in range(NFC):
                c_r(c)
            for sub in range(NSUB):
                for half in range(2):
                    bank = sub * 2 + half
                    vop(lambda e, b=bank, s=sub, h=half: e.activation(
                        r_bf[:, s * D + h * 512: s * D + (h + 1) * 512],
                        ps[b][:, 0:512], AF.Tanh, bias=0.0, scale=0.5),
                        reads=("pb%d" % bank,), writes=("r_bf",), eng="S")
            if tt + 1 < NTT:
                ln_block(tt + 1)
                sb_block(tt + 1)
            for m in range(8):
                sreg, wap = stream(wvT_sb, "wv",
                                   scr_wv.rearrange("k p d -> (k p) d")
                                   [m * 128:(m + 1) * 128, :], tt * 8 + m, q="G")
                for sub in range(NSUB):
                    for half in range(2):
                        bank = (2 + sub * 2 + half) % 8
                        mm(ps[bank][:, 0:512],
                           kact[:, m * TILE + sub * 128: m * TILE + (sub + 1) * 128],
                           wap[:, half * 512:(half + 1) * 512],
                           reads=("kact%d" % m, sreg), psreg="pb%d" % bank,
                           start=(m == 0), stop=(m == 7))
            for sub in range(NSUB):
                opar = sub % 2
                row = seq * L + t0 + sub * 128
                for half in range(2):
                    bank = (2 + sub * 2 + half) % 8
                    oa = out_sb[:, half * 512:(half + 1) * 512]
                    ra = r_bf[:, sub * D + half * 512: sub * D + (half + 1) * 512]
                    vop(lambda e, b=bank, o=oa, r=ra: e.scalar_tensor_tensor(
                        o, ps[b][:, 0:512], 0.5, r, ALU.mult, ALU.mult),
                        reads=("pb%d" % bank, "r_bf"), writes=("fb%d" % half,))
                    vop(lambda e, b=bank, o=oa: e.scalar_tensor_tensor(
                        o, ps[b][:, 0:512], 0.5, o, ALU.mult, ALU.add),
                        reads=("pb%d" % bank, "fb%d" % half), writes=("fb%d" % half,))
                    dma("G", dout[row:row + 128, half * 512:(half + 1) * 512], oa,
                        ("fb%d" % half, "or%d_%d" % (tt, sub)),
                        ("or%d_%d" % (tt, sub),), "dO", accum=True)



    fin = {}
    for name, val in sc.out_tokens:
        fin[name] = max(fin.get(name, 0), val)
    for name in sorted(fin):
        sc.raw_wait("SY", name, fin[name])
    sc.build(nc)
    nc.finalize()
    return nc


_NC_CACHE = None
TAPS = None
PH = 99
PH2 = 99


def kernel(**inputs):
    global _NC_CACHE
    cst = _host_constants(inputs)
    x = np.ascontiguousarray(inputs["L0_attn_x"], np.float32)
    if _NC_CACHE is None:
        _NC_CACHE = build_program()
    nc = _NC_CACHE
    in_maps = []
    for core in range(NCORE):
        m = {"x": x[2 * core: 2 * core + 2]}
        m.update(cst)
        in_maps.append(m)
    res = run_bass_kernel_spmd(nc, in_maps, core_ids=list(range(NCORE)))
    return np.concatenate([r["out"] for r in res.results], axis=0)


# version 11
# speedup vs baseline: 1.0297x; 1.0297x over previous
import numpy as np
import concourse.bass as bass
import concourse.mybir as mybir
from concourse.alu_op_type import AluOpType as ALU
from concourse.bass_utils import run_bass_kernel_spmd

F32 = mybir.dt.float32
BF16 = mybir.dt.bfloat16
I32 = mybir.dt.int32
AF = mybir.ActivationFunctionType

MAGIC = float(np.float32(1.5 * 2 ** 23))
TWO_PI = float(2 * np.pi)
RSQRT_MAGIC = 0x5F3759DF

NCORE = 8
BSEQ = 2
L = 2048
D = 1024
CT = 4128
DI = 2048
TILE = 512
NSUB = TILE // 128
NT = L // TILE
NTT = BSEQ * NT
NKV = 16
NGA = 16
NFC = 8
XNW = 516

ENGINES = ("T", "V", "S", "SY", "G")


class Sched:

    def __init__(self, nc):
        self.nc = nc
        self.cmds = {e: [] for e in ENGINES}
        self.seq = {e: 0 for e in ENGINES}
        self.clock = {e: {} for e in ENGINES}
        self.tokc = {}
        self.last_w = {}
        self.readers = {}
        self.chain = {e: "ch" + e for e in ENGINES}
        self.dma_use = {}
        self.dma_n = 0
        self.dma_g = 0
        self.out_tokens = []
        self.sem_names = set(self.chain.values())

    def _wait(self, eng, tok, waits):
        semk, val = tok
        cl = self.clock[eng]
        if cl.get(semk, 0) >= val:
            return
        waits.append((semk, val))
        tc = self.tokc.get(tok)
        if tc:
            for k, v in tc.items():
                if cl.get(k, 0) < v:
                    cl[k] = v
        if cl.get(semk, 0) < val:
            cl[semk] = val

    def emit(self, eng, fn, reads=(), writes=(), dma=None, first=True, last=True):
        waits = []
        toks = []
        if first:
            for r in writes:
                t = self.last_w.get(r)
                if t is not None:
                    toks.append(t)
                toks.extend(self.readers.get(r, ()))
        for r in reads:
            t = self.last_w.get(r)
            if t is not None:
                toks.append(t)
        need = {}
        for semk, val in toks:
            if need.get(semk, 0) < val:
                need[semk] = val
        for semk in sorted(need):
            self._wait(eng, (semk, need[semk]), waits)

        if dma is not None:
            if eng == "G":
                name = "dg%d" % (self.dma_g % 8)
                self.dma_g += 1
            else:
                name = "dp%d" % (self.dma_n % 56)
                self.dma_n += 1
            self.sem_names.add(name)
            prev = self.dma_use.get(name, 0)
            if prev > 0:
                self._wait(eng, (name, prev), waits)
            self.dma_use[name] = prev + 16
            tok = (name, self.dma_use[name])
            incs = [(name, 16)]
            if dma == "dO":
                self.out_tokens.append(tok)
        else:
            self.seq[eng] += 1
            tok = (self.chain[eng], self.seq[eng])
            incs = [(self.chain[eng], 1)]
        tc = dict(self.clock[eng])
        tc[tok[0]] = tok[1]
        self.tokc[tok] = tc
        self.cmds[eng].append((waits, fn, incs))

        for r in reads:
            self.readers.setdefault(r, []).append(tok)
        if last:
            for r in writes:
                self.last_w[r] = tok
                self.readers[r] = []
        return tok

    def raw_wait(self, eng, semk, val):
        self.cmds[eng].append(([(semk, val)], None, []))
        cl = self.clock[eng]
        if cl.get(semk, 0) < val:
            cl[semk] = val

    def build(self, nc):
        sems = {name: nc.alloc_semaphore(name) for name in sorted(self.sem_names)}
        with nc.Block() as block:
            def mk(engname):
                def body(eobj):
                    for waits, fn, incs in self.cmds[engname]:
                        for semk, val in waits:
                            eobj.wait_ge(sems[semk], val)
                        if fn is not None:
                            inst = fn(eobj)
                            for semk, d in incs:
                                inst = inst.then_inc(sems[semk], d)
                return body
            block.tensor(mk("T"))
            block.vector(mk("V"))
            block.scalar(mk("S"))
            block.sync(mk("SY"))
            block.gpsimd(mk("G"))
        return sems


def _host_constants(ins):
    ck = np.ascontiguousarray(ins["L0_attn_conv_k"]).astype(np.float32, copy=False)
    c = {}
    c["ipw"] = np.ascontiguousarray(ins["L0_attn_in_proj_w"], np.float32)

    diag = np.zeros((128, 32, 4, 128), np.float32)
    ar = np.arange(128)
    for cc in range(32):
        for j in range(4):
            diag[ar, cc, j, ar] = ck[j, 0, cc * 128 + ar]
    c["diag"] = diag.reshape(128, 32 * 4 * 128)

    diag_s = np.zeros((32, 4, 32), np.float32)
    a32 = np.arange(32)
    for j in range(4):
        diag_s[a32, j, a32] = ck[j, 0, 4096 + a32]
    c["diag_s"] = diag_s.reshape(32, 128)

    th = np.ascontiguousarray(ins["L0_attn_theta"], np.float32).reshape(32, 64)
    c["theta_pc"] = th.reshape(16, 2, 64).transpose(1, 2, 0).reshape(128, 16).copy()

    cb = np.ascontiguousarray(ins["L0_attn_conv_b"], np.float32)
    bd = np.zeros((128, 33), np.float32)
    bd[:, :32] = cb[:4096].reshape(32, 128).T
    bd[:32, 32] = cb[4096:4128]
    c["bdev"] = bd
    c["bs"] = cb[4096:4128].reshape(32, 1).copy()

    c["ss"] = np.ascontiguousarray(ins["L0_attn_score_scale"], np.float32).reshape(32, 1)
    c["slopes"] = np.concatenate([ins["L0_attn_decay_slopes"],
                                  ins["L0_attn_anchor_slopes"]]).astype(np.float32).reshape(32, 1)
    c["signm"] = np.array([1.0] * 28 + [-1.0] * 4, np.float32).reshape(32, 1)
    c["biasm"] = np.array([-float(L - 1)] * 28 + [0.0] * 4, np.float32).reshape(32, 1)

    c["mur"] = np.ascontiguousarray(ins["L0_ffn_mu_r"], np.float32).reshape(8, 128).T.copy()
    c["muk"] = np.ascontiguousarray(ins["L0_ffn_mu_k"], np.float32).reshape(8, 128).T.copy()
    c["dlog"] = np.ascontiguousarray(ins["L0_attn_deriv_logits"], np.float32).reshape(1, 3)

    nsv = np.ascontiguousarray(ins["L0_attn_norm_scale"], np.float32)
    c["ns2re"] = np.tile(nsv[:64], 2).reshape(128, 1)
    c["ns2im"] = np.tile(nsv[64:], 2).reshape(128, 1)

    wre = np.ascontiguousarray(ins["L0_attn_W_re"], np.float32)
    wim = np.ascontiguousarray(ins["L0_attn_W_im"], np.float32)
    bdre = np.zeros((128, 128), np.float32)
    bdre[:64, :64] = wre
    bdre[64:, 64:] = wre
    bdim = np.zeros((128, 128), np.float32)
    bdim[:64, :64] = wim
    bdim[64:, 64:] = wim
    c["bdre"] = bdre
    c["bdim"] = bdim

    sel = np.zeros((32, 16, 128), np.float32)
    for cc in range(16):
        sel[2 * cc, cc, :64] = 1.0
        sel[2 * cc + 1, cc, 64:] = 1.0
    c["sel"] = sel.reshape(32, 2048)

    selq = np.zeros((128, 16, 32), np.float32)
    for cc in range(16):
        selq[:64, cc, 2 * cc] = 1.0
        selq[64:, cc, 2 * cc + 1] = 1.0
    c["selq"] = selq.reshape(128, 512)

    c["ident"] = np.eye(128, dtype=np.float32)
    c["wr"] = np.ascontiguousarray(ins["L0_ffn_W_r"], np.float32)
    c["wk"] = np.ascontiguousarray(ins["L0_ffn_W_k"], np.float32)
    c["wv"] = np.ascontiguousarray(ins["L0_ffn_W_v"], np.float32)
    c["wout"] = np.ascontiguousarray(ins["L0_attn_out_proj_w"], np.float32)
    return c


def build_program():
    nc = bass.Bass("TRN2", target_bir_lowering=False)
    sc = Sched(nc)

    dx = nc.dram_tensor("x", [BSEQ, L, D], F32, kind="ExternalInput").ap() \
        .rearrange("s l d -> (s l) d")
    dout = nc.dram_tensor("out", [BSEQ, L, D], F32, kind="ExternalOutput").ap() \
        .rearrange("s l d -> (s l) d")
    dv = {}
    for name, shp in [
        ("ipw", [D, CT]), ("diag", [128, 16384]), ("diag_s", [32, 128]),
        ("theta_pc", [128, 16]), ("bdev", [128, 33]), ("bs", [32, 1]),
        ("ss", [32, 1]), ("slopes", [32, 1]), ("signm", [32, 1]),
        ("biasm", [32, 1]), ("mur", [128, 8]), ("muk", [128, 8]),
        ("dlog", [1, 3]), ("ns2re", [128, 1]), ("ns2im", [128, 1]),
        ("bdre", [128, 128]), ("bdim", [128, 128]), ("sel", [32, 2048]),
        ("selq", [128, 512]), ("ident", [128, 128]),
        ("wr", [D, D]), ("wk", [D, D]), ("wv", [D, D]), ("wout", [DI, D]),
    ]:
        dv[name] = nc.dram_tensor(name, shp, F32, kind="ExternalInput").ap()

    scr_ipw = nc.dram_tensor("scr_ipw", [33, 128, 1536], BF16, kind="Internal").ap()
    scr_wout = nc.dram_tensor("scr_wout", [16 * 128, D], BF16, kind="Internal").ap()
    scr_wr = nc.dram_tensor("scr_wr", [8, 128, D], BF16, kind="Internal").ap()
    scr_wk = nc.dram_tensor("scr_wk", [8, 128, D], BF16, kind="Internal").ap()
    scr_wv = nc.dram_tensor("scr_wv", [8, 128, D], BF16, kind="Internal").ap()

    A = nc.alloc_sbuf_tensor
    cmb_sb = A("cmb_sb", [128, 4 * 1536], BF16)
    diag_s_bf = A("diag_s_bf", [32, 128], BF16)
    sel_bf = A("sel_bf", [32, 2048], BF16)
    selq_bf = A("selq_bf", [128, 512], BF16)
    ident_bf = A("ident_bf", [128, 128], BF16)
    wre_bd = A("wre_bd", [128, 128], BF16)
    wim_bd = A("wim_bd", [128, 128], BF16)
    iota_c = A("iota_c", [32, TILE], F32)
    tw_sb = A("tw_sb", [32, TILE], F32)
    bias_t = A("bias_t", [32, 1], F32)
    theta_pc = A("theta_pc_sb", [128, 16], F32)
    thp = A("thp", [128, 16], F32)
    thpb = A("thpb", [128, 16], F32)
    thpb25 = A("thpb25", [128, 16], F32)
    bdev = A("bdev_sb", [128, 33], F32)
    bs_s = A("bs_s", [32, 1], F32)
    ss_s = A("ss_s", [32, 1], F32)
    slopes = A("slopes_sb", [32, 1], F32)
    signm = A("signm_sb", [32, 1], F32)
    biasm = A("biasm_sb", [32, 1], F32)
    spb = A("spb", [32, 1], F32)
    scale_v = A("scale_v", [32, 1], F32)
    bias_v = A("bias_v", [32, 1], F32)
    mur = A("mur_sb", [128, 8], F32)
    muk = A("muk_sb", [128, 8], F32)
    dlog_s = A("dlog_s", [1, 3], F32)
    e3 = A("e3", [1, 3], F32)
    ssum = A("ssum", [1, 1], F32)
    sinv = A("sinv", [1, 1], F32)
    wrow = A("wrow", [1, 3], F32)
    w_b = A("w_b", [128, 3], F32)
    nw2 = A("nw2", [128, 1], F32)
    ns2re = A("ns2re_sb", [128, 1], F32)
    ns2im = A("ns2im_sb", [128, 1], F32)
    bdstage = A("bdstage", [128, 128], F32)
    zeros32 = A("zeros32", [128, TILE], F32)
    wout_sb = A("wout_sb", [128, 2 * D], BF16)
    wrT_sb = A("wrT_sb", [128, 2 * D], BF16)
    wkT_sb = A("wkT_sb", [128, 2 * D], BF16)
    wvT_sb = A("wvT_sb", [128, 2 * D], BF16)
    x_sb = A("x_sb", [128, 4 * D], F32)
    out_sb = A("out_sb", [128, D], F32)
    xn_tm = A("xn_tm", [128, D], BF16)
    bnstat = A("bnstat", [128, 12 * NSUB], F32)
    bnout = A("bnout", [128, 2 * NSUB], F32)
    vv = A("vv", [128, NSUB], F32)
    rsv = A("rsv", [128, NSUB], F32)
    nsc1 = A("nsc1", [128, NSUB], F32)
    xn_fm = A("xn_fm", [128, NFC * XNW], BF16)
    z_ch = A("z_ch", [128, 2 * (TILE + 3)], BF16)
    z_hal = A("z_hal", [128, 33 * 3], BF16)
    z_s = A("z_s", [32, TILE + 3], BF16)
    num_re = A("num_re", [128, NKV * TILE], BF16)
    num_im = A("num_im", [128, NKV * TILE], BF16)
    sqre = A("sqre", [128, TILE], BF16)
    sqim = A("sqim", [128, TILE], BF16)
    bufA = A("bufA", [128, 2 * TILE], F32)
    bufB = A("bufB", [128, 2 * TILE], F32)
    bufC = A("bufC", [128, 3 * TILE], F32)
    bufD = A("bufD", [128, 2 * TILE], F32)
    bufE = A("bufE", [128, 2 * TILE], F32)
    bufG = A("bufG", [128, 2 * TILE], BF16)
    bufH = A("bufH", [128, 2 * TILE], BF16)
    bufF = A("bufF", [128, 3 * TILE], F32)
    rs_re = A("rs_re", [128, 2], F32)
    rs_im = A("rs_im", [128, 2], F32)
    carry_re = A("carry_re", [128, 16], F32)
    carry_im = A("carry_im", [128, 16], F32)
    pc_b = A("pc_b", [32, TILE], F32)
    p_b = A("p_b", [32, TILE], F32)
    pw_bf = A("pw_bf", [32, TILE], BF16)
    den_r = A("den_r", [32, TILE], F32)
    den_c = A("den_c", [32, 1], F32)
    g_f = A("g_f", [32, TILE], F32)
    g_bf = A("g_bf", [32, TILE], BF16)
    gate_b = A("gate_b", [128, NGA * TILE], BF16)
    mbuf = A("mbuf", [128, 2 * TILE], BF16)
    dbuf = A("dbuf", [128, TILE], BF16)
    xr_b = A("xr_b", [128, 2 * TILE], BF16)
    xk_b = A("xk_b", [128, 8 * TILE], BF16)
    kr_b = A("kr_b", [128, TILE], F32)
    kact = A("kact", [128, 8 * TILE], BF16)
    r_bf = A("r_bf", [128, 4 * D], BF16)

    ps = [nc.alloc_psum_tensor("ps%d" % i, [128, 512], F32) for i in range(8)]

    def dma(q, out_ap, in_ap, reads, writes, sem, first=True, last=True, accum=False):
        eng = "SY" if q == "SY" else "G"
        if accum:
            fn = lambda e, o=out_ap, i=in_ap: e.dma_start(o, i, accum_op=ALU.add)
        else:
            fn = lambda e, o=out_ap, i=in_ap: e.dma_start(o, i)
        return sc.emit(eng, fn, reads=reads, writes=writes, dma=sem,
                       first=first, last=last)

    def vop(fn, reads, writes, eng="V", first=True, last=True):
        return sc.emit(eng, fn, reads=reads, writes=writes, first=first, last=last)

    def mm(out_ap, lhsT_ap, rhs_ap, reads, psreg, start, stop):
        return sc.emit(
            "T",
            lambda e, o=out_ap, l=lhsT_ap, r=rhs_ap, st=start, sp=stop:
                e.matmul(o, l, r, start=st, stop=sp),
            reads=reads, writes=(psreg,), first=start, last=stop)

    def rsqrt_chain(dst, src, tmp, srcreg, dstreg, tmpreg):
        vop(lambda e: e.tensor_scalar(dst.bitcast(I32), src.bitcast(I32), 1, None,
                                      ALU.logical_shift_right),
            reads=(srcreg,), writes=(dstreg,))
        vop(lambda e: e.tensor_scalar(dst.bitcast(I32), dst.bitcast(I32), -1,
                                      RSQRT_MAGIC, ALU.mult, ALU.add),
            reads=(dstreg,), writes=(dstreg,))
        for _ in range(2):
            vop(lambda e: e.scalar_tensor_tensor(tmp, dst, 1.0, dst, ALU.mult, ALU.mult),
                reads=(dstreg,), writes=(tmpreg,))
            vop(lambda e: e.scalar_tensor_tensor(tmp, tmp, -0.5, src, ALU.mult, ALU.mult),
                reads=(tmpreg, srcreg), writes=(tmpreg,))
            vop(lambda e: e.tensor_scalar(tmp, tmp, 1.5, None, ALU.add),
                reads=(tmpreg,), writes=(tmpreg,))
            vop(lambda e: e.scalar_tensor_tensor(dst, dst, 1.0, tmp, ALU.mult, ALU.mult),
                reads=(dstreg, tmpreg), writes=(dstreg,))

    tap_outs = {}

    def tap(name, ap, regs):
        if TAPS is None or name not in TAPS or name in tap_outs:
            return
        shp = list(ap.shape)
        dt_ = ap.dtype
        t = nc.dram_tensor("dbg_" + name, shp, dt_, kind="ExternalOutput").ap()
        tap_outs[name] = t
        dma("SY", t, ap, tuple(regs), ("dbg" + name,), "dO")

    for chp in range(33):
        wq = 128 if chp < 32 else 32
        dma("G", scr_ipw[chp][:, 0:1024].rearrange("p (k q) -> p k q", k=8)[:, :, 0:wq],
            dv["ipw"].rearrange("(k p) c -> p k c", p=128)
            [:, :, chp * 128: chp * 128 + wq],
            (), ("scripw",), "dG", first=False, last=(chp == 32))
    for chp in range(32):
        dma("G", scr_ipw[chp][:, 1024:1536], dv["diag"][:, chp * 512:(chp + 1) * 512],
            (), ("scripw",), "dG", first=(chp == 0), last=False)
    dma("G", diag_s_bf[:], dv["diag_s"], (), ("diag_s",), "dG")
    dma("G", sel_bf[:], dv["sel"], (), ("sel",), "dG")
    dma("G", selq_bf[:], dv["selq"], (), ("selq",), "dG")
    dma("G", ident_bf[:], dv["ident"], (), ("ident",), "dG")
    dma("G", scr_wout, dv["wout"], (), ("scrwout",), "dG")
    for nm, dst in [("theta_pc", theta_pc), ("bdev", bdev), ("bs", bs_s),
                    ("ss", ss_s), ("slopes", slopes), ("signm", signm),
                    ("biasm", biasm), ("mur", mur), ("muk", muk),
                    ("dlog", dlog_s), ("ns2re", ns2re), ("ns2im", ns2im)]:
        dma("SY", dst[:], dv[nm], (), (nm,), "dX")
    dma("SY", bdstage[:], dv["bdre"], (), ("bdstage",), "dX")

    vop(lambda e: e.memset(zeros32[:], 0.0), reads=(), writes=("zeros",))
    vop(lambda e: e.tensor_scalar(thp[:], theta_pc[:], 1.0 / TWO_PI, None, ALU.mult),
        reads=("theta_pc",), writes=("thp",))
    vop(lambda e: e.tensor_mul(thpb[:], thp[:], bdev[:, 0:16]),
        reads=("thp", "bdev"), writes=("thpb",))
    vop(lambda e: e.tensor_scalar(thpb25[:], thpb[:], 0.25, None, ALU.add),
        reads=("thpb",), writes=("thpb25",))
    vop(lambda e: e.tensor_scalar(wre_bd[:], bdstage[:], ns2re[:], -1.0, ALU.mult, ALU.mult),
        reads=("bdstage", "ns2re"), writes=("wrebd",))
    dma("SY", bdstage[:], dv["bdim"], ("bdstage",), ("bdstage",), "dX")
    vop(lambda e: e.tensor_scalar(wim_bd[:], bdstage[:], ns2im[:], -1.0, ALU.mult, ALU.mult),
        reads=("bdstage", "ns2im"), writes=("wimbd",))

    vop(lambda e: e.activation(spb[:], slopes[:], AF.Exp, bias=0.0, scale=1.0),
        reads=("slopes",), writes=("spb",), eng="S")
    vop(lambda e: e.tensor_scalar(spb[:], spb[:], 1.0, None, ALU.add),
        reads=("spb",), writes=("spb",))
    vop(lambda e: e.activation(spb[:], spb[:], AF.Ln, bias=0.0, scale=1.0),
        reads=("spb",), writes=("spb",), eng="S")
    vop(lambda e: e.tensor_mul(scale_v[:], spb[:], signm[:]),
        reads=("spb", "signm"), writes=("scalev",))
    vop(lambda e: e.tensor_mul(bias_v[:], spb[:], biasm[:]),
        reads=("spb", "biasm"), writes=("biasv",))
    vop(lambda e: e.iota(iota_c[:], [[1, TILE]], channel_multiplier=0,
                         allow_small_or_imprecise_dtypes=True),
        reads=(), writes=("iotac",), eng="G")
    vop(lambda e: e.activation(e3[:], dlog_s[:], AF.Exp, bias=0.0, scale=1.0),
        reads=("dlog",), writes=("e3",), eng="S")
    vop(lambda e: e.reduce_sum(ssum[:], e3[:], axis=mybir.AxisListType.X),
        reads=("e3",), writes=("ssum",))
    vop(lambda e: e.reciprocal(sinv[:], ssum[:]), reads=("ssum",), writes=("sinv",))
    vop(lambda e: e.tensor_scalar(wrow[:], e3[:], sinv[:], None, ALU.mult),
        reads=("e3", "sinv"), writes=("wrow",))
    dma("SY", w_b[0:1, :], wrow[:], ("wrow",), ("w_b",), "dX")
    kk2 = 1
    while kk2 < 128:
        dma("SY", w_b[kk2:2 * kk2, :], w_b[0:kk2, :], ("w_b",), ("w_b",), "dX")
        kk2 *= 2
    vop(lambda e: e.tensor_scalar(nw2[:], w_b[:, 2:3], -1.0, None, ALU.mult),
        reads=("w_b",), writes=("nw2",))

    for wname, src, scr in (("wr", dv["wr"], scr_wr), ("wk", dv["wk"], scr_wk),
                            ("wv", dv["wv"], scr_wv)):
        for kk in range(8):
            dma("G", kact[:, 1024:2048], src[kk * 128:(kk + 1) * 128, :],
                ("kact4", "kact5", "kact6", "kact7"),
                ("kact4", "kact5", "kact6", "kact7"), "dG")
            for t8 in range(8):
                bank = t8 // 4
                mm(ps[bank][:, (t8 % 4) * 128:(t8 % 4 + 1) * 128],
                   kact[:, 1024 + t8 * 128:1024 + (t8 + 1) * 128], ident_bf[:],
                   reads=("ident", "kact4", "kact5", "kact6", "kact7"),
                   psreg="pb%d" % bank, start=True, stop=True)
            for bank in range(2):
                vop(lambda e, b=bank: e.tensor_copy(
                    kact[:, b * 512:(b + 1) * 512], ps[b][:]),
                    reads=("pb%d" % bank,),
                    writes=("kact%d" % (2 * bank), "kact%d" % (2 * bank + 1)))
            if wname == "wk":
                dma("SY", scr.rearrange("k p d -> (k p) d")[kk * 128:(kk + 1) * 128, :],
                    kact[:, 0:1024],
                    ("kact0", "kact1", "kact2", "kact3"), ("scr" + wname,), "dX",
                    first=(kk == 0), last=(kk == 7))
            else:
                dma("SY",
                    scr[:, :, kk * 128:(kk + 1) * 128].transpose([1, 0, 2]),
                    kact[:, 0:1024].rearrange("p (m w) -> p m w", w=128),
                    ("kact0", "kact1", "kact2", "kact3"), ("scr" + wname,), "dX",
                    first=(kk == 0), last=(kk == 7))

    def xslot(gsub):
        return gsub % 4

    def x_load(tt):
        seq, t = divmod(tt, NT)
        for sub in range(NSUB):
            sl = xslot(tt * NSUB + sub)
            row = seq * L + t * TILE + sub * 128
            dma("SY", x_sb[:, sl * D:(sl + 1) * D], dx[row:row + 128, :],
                (), ("x%d" % sl,), "dX")

    def xn_chunk(kk, a, b):
        return xn_fm[:, kk * XNW + a: kk * XNW + b]

    scrw_wout = scr_wout

    def stream(buf, name, piece_ap, idx, q="SY"):
        par = idx % 2
        reg = "%s%d" % (name, par)
        dma(q, buf[:, par * D:(par + 1) * D], piece_ap, ("scr" + name,), (reg,), "dX")
        return reg, buf[:, par * D:(par + 1) * D]

    x_load(0)

    for tt in range(NTT):
        seq, t = divmod(tt, NT)
        t0 = t * TILE

        def ln_block(tt2):
            if tt2 % NT == 0:
                vop(lambda e: e.memset(
                    xn_fm[:].rearrange("p (c w) -> p c w", w=XNW)[:, :, 0:4], 0.0),
                    reads=(), writes=("xn",))
                vop(lambda e: e.memset(carry_re[:], 0.0), reads=(), writes=("c_re",))
                vop(lambda e: e.memset(carry_im[:], 0.0), reads=(), writes=("c_im",))
                vop(lambda e: e.memset(den_c[:], 0.0), reads=(), writes=("den_c",))
                vop(lambda e: e.memset(z_hal[:], 0.0), reads=(), writes=("zhal",))
            else:
                vop(lambda e: e.tensor_copy(
                    xn_fm[:].rearrange("p (c w) -> p c w", w=XNW)[:, :, 0:4],
                    xn_fm[:].rearrange("p (c w) -> p c w", w=XNW)[:, :, TILE:TILE + 4]),
                    reads=("xn",), writes=("xn",))

            if True:
                for sub in range(NSUB):
                    sl = xslot(tt2 * NSUB + sub)
                    xap = x_sb[:, sl * D:(sl + 1) * D]
                    for hh in range(2):
                        vop(lambda e, xa=xap, s=sub, h=hh: e.bn_stats(
                            bnstat[:, s * 12 + h * 6: s * 12 + (h + 1) * 6],
                            xa[:, h * 512:(h + 1) * 512]),
                            reads=("x%d" % sl,), writes=("bnst%d" % sub,),
                            first=(hh == 0), last=(hh == 1))
                    vop(lambda e, s=sub: e.bn_aggr(bnout[:, s * 2:(s + 1) * 2],
                                                   bnstat[:, s * 12:(s + 1) * 12]),
                        reads=("bnst%d" % sub,), writes=("bno%d" % sub,))
                    vop(lambda e, s=sub: e.tensor_copy(vv[:, s:s + 1],
                                                       bnout[:, s * 2 + 1:s * 2 + 2]),
                        reads=("bno%d" % sub,), writes=("vv",), first=(sub == 0),
                        last=(sub == NSUB - 1))
                vop(lambda e: e.tensor_scalar(vv[:], vv[:], 1e-5, None, ALU.add),
                    reads=("vv",), writes=("vv",))
                rsqrt_chain(rsv[:], vv[:], nsc1[:], "vv", "rsv", "lnt")
                for sub in range(NSUB):
                    sl = xslot(tt2 * NSUB + sub)
                    xap = x_sb[:, sl * D:(sl + 1) * D]
                    vop(lambda e, xa=xap, s=sub: e.tensor_scalar(
                        xn_tm[:], xa, bnout[:, s * 2:s * 2 + 1], rsv[:, s:s + 1],
                        ALU.subtract, ALU.mult),
                        reads=("x%d" % sl, "bno%d" % sub, "rsv"), writes=("xntm",))
                    for t8 in range(8):
                        bank = t8 // 4
                        mm(ps[bank][:, (t8 % 4) * 128:(t8 % 4 + 1) * 128],
                           xn_tm[:, t8 * 128:(t8 + 1) * 128], ident_bf[:],
                           reads=("xntm", "ident"), psreg="pb%d" % bank,
                           start=True, stop=True)
                    for bank in range(2):
                        cc0 = bank * 4
                        vop(lambda e, b=bank, c0=cc0, s=sub: e.tensor_copy(
                            xn_fm[:].rearrange("p (c w) -> p c w", w=XNW)
                            [:, c0:c0 + 4, 4 + s * 128:4 + (s + 1) * 128],
                            ps[b][:].rearrange("p (c w) -> p c w", w=128)),
                            reads=("pb%d" % bank,), writes=("xn",))

                if tt == 0:
                    tap("xn", xn_fm[:], ("xn",))
                    tap("xld", x_sb[:, 0:D], ("x0",))
                    tap("bno", bnout[:], ("bno0", "bno1"))
                    tap("rsv", rsv[:], ("rsv",))
                    tap("bnst", bnstat[:], ("bnst0", "bnst1"))
                    tap("vv", vv[:], ("vv",))
                    tap("xntm", xn_tm[:], ("xntm",))


        if tt == 0:
            ln_block(0)

        ipw_seq = [32] + [p // 2 if p % 2 == 0 else 16 + p // 2 for p in range(32)]

        def ipw_stream(si_):
            ch = ipw_seq[si_]
            sl4 = si_ % 4
            dma("SY", cmb_sb[:, sl4 * 1536:(sl4 + 1) * 1536], scr_ipw[ch],
                ("scripw",), ("ipwp%d" % sl4,), "dX")

        def ipw_piece(si_):
            sl4 = si_ % 4
            return "ipwp%d" % sl4, cmb_sb[:, sl4 * 1536: sl4 * 1536 + 1024]

        def diag_piece(p):
            si_ = p + 1
            sl4 = si_ % 4
            return "ipwp%d" % sl4, cmb_sb[:, sl4 * 1536 + 1024:(sl4 + 1) * 1536]

        def sb_block(tt2):
            if True:
                if PH2 > 0:
                    for si_ in range(4):
                        ipw_stream(si_)

                if PH2 > 1:
                    sreg0, sap0 = ipw_piece(0)
                    for kk in range(8):
                        mm(ps[6][0:32, 0:TILE], sap0[:, kk * 128: kk * 128 + 32],
                           xn_chunk(kk, 4, XNW), reads=(sreg0, "xn"), psreg="pb6",
                           start=(kk == 0), stop=(kk == 7))

                if PH2 > 2:
                    vop(lambda e: e.tensor_copy(z_s[:, 0:3], z_hal[0:32, 96:99]),
                        reads=("zhal",), writes=("zhl_s",))
                    vop(lambda e: e.activation(z_s[:, 3:TILE + 3], ps[6][0:32, 0:TILE], AF.Copy,
                                               bias=0.0, scale=1.0),
                        reads=("pb6",), writes=("z_s",), eng="S")
                    vop(lambda e: e.activation(z_hal[0:32, 96:99], ps[6][0:32, TILE - 3:TILE],
                                               AF.Copy, bias=0.0, scale=1.0),
                        reads=("pb6",), writes=("zhal",), eng="S")

                if PH2 > 3:
                    for j in range(4):
                        mm(ps[7][0:32, 0:TILE], diag_s_bf[:, j * 32:(j + 1) * 32],
                           z_s[:, j:j + TILE], reads=("diag_s", "z_s", "zhl_s"), psreg="pb7",
                           start=(j == 0), stop=(j == 3))

                if PH2 > 4:
                    vop(lambda e: e.tensor_scalar(pc_b[:], ps[7][0:32, 0:TILE], bs_s[:], ss_s[:],
                                                  ALU.add, ALU.mult),
                        reads=("pb7", "bs", "ss"), writes=("pc",))
                    vop(lambda e: e.tensor_scalar(pc_b[:], pc_b[:], 20.0, -20.0, ALU.min, ALU.max),
                        reads=("pc",), writes=("pc",))

                if PH2 > 5:
                    vop(lambda e, tq=float((tt2 % NT) * TILE): e.scalar_tensor_tensor(
                        bias_t[:], scale_v[:], tq, bias_v[:], ALU.mult, ALU.add),
                        reads=("scalev", "biasv"), writes=("biast",))
                    vop(lambda e: e.activation(p_b[:], pc_b[:], AF.Exp, bias=0.0, scale=1.0),
                        reads=("pc",), writes=("p_b",), eng="S")
                    vop(lambda e: e.activation(tw_sb[:], iota_c[:], AF.Exp, bias=bias_t[:],
                                               scale=scale_v[:]),
                        reads=("iotac", "biast", "scalev"), writes=("tw",), eng="S")

                if PH2 > 6:
                    vop(lambda e: e.tensor_mul(pw_bf[:], p_b[:], tw_sb[:]),
                        reads=("p_b", "tw"), writes=("pw",))

                if PH2 > 7:
                    vop(lambda e: e.tensor_tensor_scan(den_r[:], zeros32[0:32, :], pw_bf[:],
                                                       den_c[:], ALU.add, ALU.add),
                        reads=("zeros", "pw", "den_c"), writes=("den",))
                    vop(lambda e: e.tensor_copy(den_c[:], den_r[:, TILE - 1:TILE]),
                        reads=("den",), writes=("den_c",))
                    vop(lambda e: e.tensor_scalar(den_r[:], den_r[:], 1e-4, None, ALU.max),
                        reads=("den",), writes=("den",))

                    if tt == 0:
                        tap("zs", z_s[:], ("z_s", "zhl_s"))
                        tap("tw", tw_sb[:], ("tw",))
                        tap("p", p_b[:], ("p_b",))
                        tap("pw", pw_bf[:], ("pw",))
                        tap("den", den_r[:], ("den",))



        sb_block(tt)

        if PH > 2:
            for c in range(NFC):
                db = dbuf[:, 0:TILE]
                vop(lambda e, o=db, cc=c: e.tensor_sub(o, xn_chunk(cc, 4, XNW),
                                                       xn_chunk(cc, 3, XNW - 1)),
                    reads=("xn",), writes=("db",))
                vop(lambda e, i=db, cc=c: e.scalar_tensor_tensor(
                    xk_b[:, cc * TILE:(cc + 1) * TILE], i, muk[:, cc:cc + 1],
                    xn_chunk(cc, 3, XNW - 1), ALU.mult, ALU.add),
                    reads=("db", "muk", "xn"), writes=("xkq%d" % c,))

            def k_group(m):
                sreg, wap = stream(wkT_sb, "wk",
                                   scr_wk.rearrange("k p d -> (k p) d")
                                   [m * 128:(m + 1) * 128, :], tt * 8 + m, q="G")
                for c in range(NFC):
                    mm(ps[4][:, 0:TILE], wap[:, c * 128:(c + 1) * 128],
                       xk_b[:, c * TILE:(c + 1) * TILE],
                       reads=(sreg, "xkq%d" % c), psreg="pb4",
                       start=(c == 0), stop=(c == NFC - 1))
                vop(lambda e: e.tensor_scalar(kr_b[:, 0:TILE], ps[4][:, 0:TILE], 0.0,
                                              None, ALU.max),
                    reads=("pb4",), writes=("kr",))
                vop(lambda e, mq=m: e.activation(
                    kact[:, mq * TILE:(mq + 1) * TILE], kr_b[:, 0:TILE], AF.Square,
                    bias=0.0, scale=1.0),
                    reads=("kr",), writes=("kact%d" % m,), eng="S")


        if PH > 3:
            def pa_ctx(p):
                c = p // 2
                kind = p % 2
                pi = p % 2
                cpi = c % 2
                p3 = c % 3
                d = dict(
                    p=p, c=c, kind=kind, pi=pi, cpi=cpi,
                    zb=ps[pi], zreg="pb%d" % pi, cb=ps[2 + pi],
                    creg="pb%d" % (2 + pi),
                    zc=z_ch[:, pi * (TILE + 3):(pi + 1) * (TILE + 3)])
                if kind == 0:
                    d.update(
                        pwb=ps[6 + cpi], pwreg="pb%d" % (6 + cpi),
                        bA=bufA[:, cpi * TILE:(cpi + 1) * TILE],
                        bB=bufB[:, cpi * TILE:(cpi + 1) * TILE],
                        bC=bufC[:, p3 * TILE:(p3 + 1) * TILE],
                        bD=bufD[:, cpi * TILE:(cpi + 1) * TILE],
                        bE=bufE[:, cpi * TILE:(cpi + 1) * TILE],
                        bF=bufF[:, p3 * TILE:(p3 + 1) * TILE],
                        bG=bufG[:, cpi * TILE:(cpi + 1) * TILE],
                        bH=bufH[:, cpi * TILE:(cpi + 1) * TILE],
                        sqr=sqre[:, 0:TILE], sqi=sqim[:, 0:TILE],
                        rA="bA%d" % cpi, rB="bB%d" % cpi, rC="bC%d" % p3,
                        rD="bD%d" % cpi, rE="bE%d" % cpi, rF="bF%d" % p3,
                        rG="bG%d" % cpi, rH="bH%d" % cpi)
                return d

            def paT_inproj(k):
                p = k["p"]
                sreg, sap = ipw_piece(p + 1)
                for kk in range(8):
                    mm(k["zb"][:, 0:TILE], sap[:, kk * 128:(kk + 1) * 128],
                       xn_chunk(kk, 4, XNW), reads=(sreg, "xn"), psreg=k["zreg"],
                       start=(kk == 0), stop=(kk == 7))

            def paS_zevac(k):
                ch = k["c"] if k["kind"] == 0 else 16 + k["c"]
                vop(lambda e, z=k["zc"], c3=ch: e.tensor_copy(z[:, 0:3],
                                                              z_hal[:, c3 * 3:(c3 + 1) * 3]),
                    reads=("zhal",), writes=("zhl%d" % k["pi"],))
                vop(lambda e, z=k["zc"], b=k["zb"]: e.activation(
                    z[:, 3:TILE + 3], b[:, 0:TILE], AF.Copy, bias=0.0, scale=1.0),
                    reads=(k["zreg"],), writes=("zch%d" % k["pi"],), eng="S")
                vop(lambda e, b=k["zb"], c3=ch: e.activation(
                    z_hal[:, c3 * 3:(c3 + 1) * 3], b[:, TILE - 3:TILE],
                    AF.Copy, bias=0.0, scale=1.0),
                    reads=(k["zreg"],), writes=("zhal",), eng="S")
                if tt == 0 and k["p"] == 0:
                    tap("z0", k["zc"], ("zch0", "zhl0"))

            def paT_conv(k):
                dreg, dap = diag_piece(k["p"])
                for j in range(4):
                    mm(k["cb"][:, 0:TILE], dap[:, j * 128:(j + 1) * 128],
                       k["zc"][:, j:j + TILE],
                       reads=(dreg, "zch%d" % k["pi"], "zhl%d" % k["pi"]),
                       psreg=k["creg"], start=(j == 0), stop=(j == 3))
                if k["p"] + 4 < 33:
                    ipw_stream(k["p"] + 4)

            def paS_extract(k):
                if k["kind"] == 1:
                    c = k["c"]
                    vop(lambda e, b=k["cb"], cc=16 + c: e.activation(
                        gate_b[:, c * TILE:(c + 1) * TILE], b[:, 0:TILE], AF.Silu,
                        bias=bdev[:, cc:cc + 1], scale=1.0),
                        reads=(k["creg"], "bdev"), writes=("gatef%d" % c,), eng="S")
                    return
                c = k["c"]
                vop(lambda e, b=k["cb"], o=k["bA"], cc=c: e.activation(
                    o, b[:, 0:TILE], AF.Identity, bias=thpb[:, cc:cc + 1],
                    scale=thp[:, cc:cc + 1]),
                    reads=(k["creg"], "thpb", "thp"), writes=(k["rA"],), eng="S")
                vop(lambda e, b=k["cb"], o=k["bB"], cc=c: e.activation(
                    o, b[:, 0:TILE], AF.Identity, bias=thpb25[:, cc:cc + 1],
                    scale=thp[:, cc:cc + 1]),
                    reads=(k["creg"], "thpb25", "thp"), writes=(k["rB"],), eng="S")
                vop(lambda e, b=k["cb"], o=k["bC"], cc=c: e.activation(
                    o, b[:, 0:TILE], AF.Identity, bias=bdev[:, cc:cc + 1], scale=1.0),
                    reads=(k["creg"], "bdev"), writes=(k["rC"],), eng="S")
                if tt == 0 and k["p"] == 0:
                    tap("xv0", k["bC"], (k["rC"],))
                vop(lambda e, i=k["bC"], o=k["bF"]: e.activation(o, i, AF.Square,
                                                                 bias=0.0, scale=1.0),
                    reads=(k["rC"],), writes=(k["rF"],), eng="S")
                vop(lambda e, i=k["bC"]: e.activation(i, i, AF.Identity,
                                                      bias=w_b[:, 0:1], scale=w_b[:, 1:2]),
                    reads=(k["rC"], "w_b"), writes=(k["rC"],), eng="S")

            def paVT_magic(k):
                if k["kind"] == 1:
                    return
                c = k["c"]
                vop(lambda e, i=k["bA"], o=k["bD"]: e.activation(o, i, AF.Copy,
                                                                 bias=MAGIC, scale=1.0),
                    reads=(k["rA"],), writes=(k["rD"],), eng="S")
                vop(lambda e, i=k["bB"], o=k["bE"]: e.activation(o, i, AF.Copy,
                                                                 bias=MAGIC, scale=1.0),
                    reads=(k["rB"],), writes=(k["rE"],), eng="S")
                vop(lambda e, i=k["bD"], r=k["bA"]: e.scalar_tensor_tensor(
                    i, i, MAGIC, r, ALU.subtract, ALU.subtract),
                    reads=(k["rD"], k["rA"]), writes=(k["rD"],))
                vop(lambda e, i=k["bE"], r=k["bB"]: e.scalar_tensor_tensor(
                    i, i, MAGIC, r, ALU.subtract, ALU.subtract),
                    reads=(k["rE"], k["rB"]), writes=(k["rE"],))
                mm(k["pwb"][:, 0:TILE], sel_bf[:, c * 128:(c + 1) * 128], pw_bf[:],
                   reads=("sel", "pw"), psreg=k["pwreg"], start=True, stop=True)

            def paS_trig(k):
                if k["kind"] == 1:
                    return
                vop(lambda e, i=k["bD"]: e.activation(i, i, AF.Sin, bias=0.0, scale=TWO_PI),
                    reads=(k["rD"],), writes=(k["rD"],), eng="S")
                vop(lambda e, i=k["bE"]: e.activation(i, i, AF.Sin, bias=0.0, scale=TWO_PI),
                    reads=(k["rE"],), writes=(k["rE"],), eng="S")

            def paV_poly(k):
                if k["kind"] == 1:
                    return
                c = k["c"]
                vop(lambda e, i=k["bF"], t1=k["bC"]: e.scalar_tensor_tensor(
                    i, i, nw2[:], t1, ALU.mult, ALU.add),
                    reads=(k["rF"], "nw2", k["rC"]), writes=(k["rF"],))
                if tt == 0 and c == 0:
                    tap("sin0", k["bD"], (k["rD"],))
                    tap("t10", k["bC"], (k["rC"],))
                vop(lambda e, i=k["bF"], b=k["pwb"]: e.tensor_mul(i, i, b[:, 0:TILE]),
                    reads=(k["rF"], k["pwreg"]), writes=(k["rF"],))
                vop(lambda e, i=k["bF"], cn=k["bE"], o=k["bG"], p=k["cpi"]: e.scalar_tensor_tensor(
                    o, i, 1.0, cn, ALU.mult, ALU.mult, accum_out=rs_re[:, p:p + 1]),
                    reads=(k["rF"], k["rE"]), writes=(k["rG"], "rsre%d" % k["cpi"]))
                vop(lambda e, i=k["bF"], sn=k["bD"], o=k["bH"], p=k["cpi"]: e.scalar_tensor_tensor(
                    o, i, 1.0, sn, ALU.mult, ALU.mult, accum_out=rs_im[:, p:p + 1]),
                    reads=(k["rF"], k["rD"]), writes=(k["rH"], "rsim%d" % k["cpi"]))

            def paV_scan(k):
                if k["kind"] == 1:
                    return
                c = k["c"]
                vop(lambda e, a=k["bG"], cc=c: e.tensor_tensor_scan(
                    num_re[:, cc * TILE:(cc + 1) * TILE], zeros32[:], a,
                    carry_re[:, cc:cc + 1], ALU.add, ALU.add),
                    reads=("zeros", k["rG"], "c_re"), writes=("numre%d" % c,))
                vop(lambda e, a=k["bH"], cc=c: e.tensor_tensor_scan(
                    num_im[:, cc * TILE:(cc + 1) * TILE], zeros32[:], a,
                    carry_im[:, cc:cc + 1], ALU.add, ALU.add),
                    reads=("zeros", k["rH"], "c_im"), writes=("numim%d" % c,))
                vop(lambda e, cc=c, p=k["cpi"]: e.tensor_add(
                    carry_re[:, cc:cc + 1], carry_re[:, cc:cc + 1], rs_re[:, p:p + 1]),
                    reads=("c_re", "rsre%d" % k["cpi"]), writes=("c_re",))
                vop(lambda e, cc=c, p=k["cpi"]: e.tensor_add(
                    carry_im[:, cc:cc + 1], carry_im[:, cc:cc + 1], rs_im[:, p:p + 1]),
                    reads=("c_im", "rsim%d" % k["cpi"]), writes=("c_im",))

            def paS_sq(k):
                if k["kind"] == 1:
                    return
                c = k["c"]
                vop(lambda e, cc=c, o=k["sqr"]: e.activation(
                    o, num_re[:, cc * TILE:(cc + 1) * TILE], AF.Square, bias=0.0, scale=1.0),
                    reads=("numre%d" % c,), writes=("sqre",), eng="S")
                vop(lambda e, cc=c, o=k["sqi"]: e.activation(
                    o, num_im[:, cc * TILE:(cc + 1) * TILE], AF.Square, bias=0.0, scale=1.0),
                    reads=("numim%d" % c,), writes=("sqim",), eng="S")

            def paT_ssel(k):
                if k["kind"] == 1:
                    return
                c = k["c"]
                mm(ps[5][0:32, 0:TILE], selq_bf[:, c * 32:(c + 1) * 32], k["sqr"],
                   reads=("selq", "sqre"), psreg="pb5",
                   start=(c == 0), stop=False)
                mm(ps[5][0:32, 0:TILE], selq_bf[:, c * 32:(c + 1) * 32], k["sqi"],
                   reads=("selq", "sqim"), psreg="pb5",
                   start=False, stop=(c == NKV - 1))

            pa_stages = [paT_inproj, paS_zevac, paT_conv, paS_extract, paVT_magic,
                         paS_trig, paV_poly, paV_scan, paS_sq, paT_ssel]
            NP = 32
            ctxs = {p: pa_ctx(p) for p in range(NP)}
            ns_ = len(pa_stages)
            for i in range(NP + ns_ - 1):
                for si in range(ns_ - 1, -1, -1):
                    p = i - si
                    if 0 <= p < NP:
                        pa_stages[si](ctxs[p])
                if i >= 6 and (i - 6) % 4 == 0 and (i - 6) // 4 < 8:
                    k_group((i - 6) // 4)


        if PH > 4:
            vop(lambda e: e.scalar_tensor_tensor(pc_b[:], den_r[:], 1e-5, den_r[:],
                                                 ALU.mult, ALU.mult),
                reads=("den",), writes=("pc",))
            vop(lambda e: e.scalar_tensor_tensor(p_b[:], ps[5][0:32, 0:TILE],
                                                 1.0 / 128.0, pc_b[:], ALU.mult, ALU.add),
                reads=("pb5", "pc"), writes=("p_b",))
            rsqrt_chain(g_f[:], p_b[:], bufF[0:32, 0:TILE], "p_b", "g_f", "bF0")
            vop(lambda e: e.tensor_copy(g_bf[:], g_f[:]), reads=("g_f",), writes=("g_bf",))

            if tt == 0:
                tap("numre", num_re[:], tuple("numre%d" % i for i in range(16)))
                tap("numim", num_im[:], tuple("numim%d" % i for i in range(16)))
                tap("g", g_f[:], ("g_f",))

        if PH > 5:
            def b_ctx(c):
                cpi = c % 2
                return dict(c=c, cpi=cpi, yb=ps[2 + cpi], yreg="pb%d" % (2 + cpi),
                            gb=ps[4 + cpi], greg="pb%d" % (4 + cpi),
                            mb=mbuf[:, cpi * TILE:(cpi + 1) * TILE])

            def bT_wp(k):
                c = k["c"]
                mm(k["yb"][:, 0:TILE], wre_bd[:], num_re[:, c * TILE:(c + 1) * TILE],
                   reads=("wrebd", "numre%d" % c), psreg=k["yreg"], start=True, stop=False)
                mm(k["yb"][:, 0:TILE], wim_bd[:], num_im[:, c * TILE:(c + 1) * TILE],
                   reads=("wimbd", "numim%d" % c), psreg=k["yreg"], start=False, stop=True)
                mm(k["gb"][:, 0:TILE], sel_bf[:, c * 128:(c + 1) * 128], g_bf[:],
                   reads=("sel", "g_bf"), psreg=k["greg"], start=True, stop=True)

            def bV_yg(k):
                c = k["c"]
                vop(lambda e, m=k["mb"], b=k["gb"], cc=c: e.tensor_mul(
                    m, gate_b[:, cc * TILE:(cc + 1) * TILE], b[:, 0:TILE]),
                    reads=("gatef%d" % c, k["greg"]), writes=("mb%d" % k["cpi"],))
                vop(lambda e, m=k["mb"], b=k["yb"], cc=c: e.tensor_mul(
                    num_re[:, cc * TILE:(cc + 1) * TILE], b[:, 0:TILE], m),
                    reads=("mb%d" % k["cpi"], k["yreg"], "numre%d" % c), writes=("numre%d" % c,))

            ATTNB = [0, 1, 6, 7]

            b_stages = [bT_wp, bV_yg]
            bctxs = {c: b_ctx(c) for c in range(NGA)}
            nb_ = len(b_stages)
            for i in range(NGA + nb_ - 1):
                for si in range(nb_ - 1, -1, -1):
                    c = i - si
                    if 0 <= c < NGA:
                        b_stages[si](bctxs[c])


        if PH > 6:
            for c in range(NGA):
                sreg, wap = stream(wout_sb, "wout", scrw_wout[c * 128:(c + 1) * 128, :],
                                   tt * 16 + c, q="G")
                for sub in range(NSUB):
                    for half in range(2):
                        bank = sub * 2 + half
                        mm(ps[bank][:, 0:512],
                           num_re[:, c * TILE + sub * 128: c * TILE + (sub + 1) * 128],
                           wap[:, half * 512:(half + 1) * 512],
                           reads=("numre%d" % c, sreg), psreg="pb%d" % bank,
                           start=(c == 0), stop=(c == NGA - 1))
            for sub in range(NSUB):
                sl = xslot(tt * NSUB + sub)
                for half in range(2):
                    bank = sub * 2 + half
                    vop(lambda e, b=bank, s=sl, h=half: e.scalar_tensor_tensor(
                        x_sb[:, s * D + h * 512: s * D + (h + 1) * 512],
                        ps[b][:, 0:512], 1.0,
                        x_sb[:, s * D + h * 512: s * D + (h + 1) * 512],
                        ALU.mult, ALU.add),
                        reads=("pb%d" % bank, "x%d" % sl), writes=("x%d" % sl,))

            for sub in range(NSUB):
                sl = xslot(tt * NSUB + sub)
                row = seq * L + t0 + sub * 128
                dma("SY", dout[row:row + 128, :], x_sb[:, sl * D:(sl + 1) * D],
                    ("x%d" % sl,), ("or%d_%d" % (tt, sub),), "dO")
            if tt + 1 < NTT:
                x_load(tt + 1)

        if PH > 7:
            def c_r(c):
                pi = c % 2
                db = dbuf[:, 0:TILE]
                xrp = xr_b[:, pi * TILE:(pi + 1) * TILE]
                sreg, wap = stream(wrT_sb, "wr",
                                   scr_wr.rearrange("m p d -> (m p) d")
                                   [c * 128:(c + 1) * 128, :], tt * 8 + c, q="G")
                vop(lambda e, o=db, cc=c: e.tensor_sub(o, xn_chunk(cc, 4, XNW),
                                                       xn_chunk(cc, 3, XNW - 1)),
                    reads=("xn",), writes=("db",))
                vop(lambda e, i=db, o=xrp, cc=c: e.scalar_tensor_tensor(
                    o, i, mur[:, cc:cc + 1], xn_chunk(cc, 3, XNW - 1), ALU.mult, ALU.add),
                    reads=("db", "mur", "xn"), writes=("xr%d" % pi,))
                for sub in range(NSUB):
                    for half in range(2):
                        bank = sub * 2 + half
                        mm(ps[bank][:, 0:512], xrp[:, sub * 128:(sub + 1) * 128],
                           wap[:, half * 512:(half + 1) * 512],
                           reads=("xr%d" % pi, sreg), psreg="pb%d" % bank,
                           start=(c == 0), stop=(c == NFC - 1))

            for c in range(NFC):
                c_r(c)
            for sub in range(NSUB):
                for half in range(2):
                    bank = sub * 2 + half
                    vop(lambda e, b=bank, s=sub, h=half: e.activation(
                        r_bf[:, s * D + h * 512: s * D + (h + 1) * 512],
                        ps[b][:, 0:512], AF.Tanh, bias=0.0, scale=0.5),
                        reads=("pb%d" % bank,), writes=("r_bf",), eng="S")
            if tt + 1 < NTT:
                ln_block(tt + 1)
            for m in range(8):
                sreg, wap = stream(wvT_sb, "wv",
                                   scr_wv.rearrange("k p d -> (k p) d")
                                   [m * 128:(m + 1) * 128, :], tt * 8 + m, q="G")
                for sub in range(NSUB):
                    for half in range(2):
                        bank = (2 + sub * 2 + half) % 8
                        mm(ps[bank][:, 0:512],
                           kact[:, m * TILE + sub * 128: m * TILE + (sub + 1) * 128],
                           wap[:, half * 512:(half + 1) * 512],
                           reads=("kact%d" % m, sreg), psreg="pb%d" % bank,
                           start=(m == 0), stop=(m == 7))
            for sub in range(NSUB):
                opar = sub % 2
                row = seq * L + t0 + sub * 128
                for half in range(2):
                    bank = (2 + sub * 2 + half) % 8
                    oa = out_sb[:, half * 512:(half + 1) * 512]
                    ra = r_bf[:, sub * D + half * 512: sub * D + (half + 1) * 512]
                    vop(lambda e, b=bank, o=oa, r=ra: e.scalar_tensor_tensor(
                        o, ps[b][:, 0:512], 0.5, r, ALU.mult, ALU.mult),
                        reads=("pb%d" % bank, "r_bf"), writes=("fb%d" % half,))
                    vop(lambda e, b=bank, o=oa: e.scalar_tensor_tensor(
                        o, ps[b][:, 0:512], 0.5, o, ALU.mult, ALU.add),
                        reads=("pb%d" % bank, "fb%d" % half), writes=("fb%d" % half,))
                    dma("G", dout[row:row + 128, half * 512:(half + 1) * 512], oa,
                        ("fb%d" % half, "or%d_%d" % (tt, sub)),
                        ("or%d_%d" % (tt, sub),), "dO", accum=True)



    fin = {}
    for name, val in sc.out_tokens:
        fin[name] = max(fin.get(name, 0), val)
    for name in sorted(fin):
        sc.raw_wait("SY", name, fin[name])
    sc.build(nc)
    nc.finalize()
    return nc


_NC_CACHE = None
TAPS = None
PH = 99
PH2 = 99


def kernel(**inputs):
    global _NC_CACHE
    cst = _host_constants(inputs)
    x = np.ascontiguousarray(inputs["L0_attn_x"], np.float32)
    if _NC_CACHE is None:
        _NC_CACHE = build_program()
    nc = _NC_CACHE
    in_maps = []
    for core in range(NCORE):
        m = {"x": x[2 * core: 2 * core + 2]}
        m.update(cst)
        in_maps.append(m)
    res = run_bass_kernel_spmd(nc, in_maps, core_ids=list(range(NCORE)))
    return np.concatenate([r["out"] for r in res.results], axis=0)
